# version 13 (final)
# baseline (speedup 1.0000x reference)
import contextlib
import numpy as np

import concourse.bass as bass
import concourse.mybir as mybir
from concourse.bass_utils import run_bass_kernel_spmd

F32 = mybir.dt.float32
BF16 = mybir.dt.bfloat16
AF = mybir.ActivationFunctionType
ALU = mybir.AluOpType
AXL = mybir.AxisListType

NCORES = 8
D = 1024
HC = 8
T = 2048
NT = 16
NTC = 4
CW = 512
NSW = 8
SW = 256
NH = 16
IE = 1408
IC = 11
NU = 10
RING = 13
EPS_IN = 1e-6
EPS_OUT = 1e-5

ENGINES = ("tensor", "vector", "scalar", "gpsimd", "sync")
STAGE = 2


class B:
    def __init__(self, nc, ctx):
        self.nc = nc
        self.ops = {e: [] for e in ENGINES}
        self.sems = {}
        self.cnt = {}
        self.waited = {}
        names = ["TT", "VV", "SS", "GG", "LG", "LS", "CC", "LX", "LXB", "LXC",
                 "LWK", "LWV", "LWKH", "LWVH",
                 "LWQ", "LWO", "LBK", "LMISC", "LSEL"] + [f"LW{j}" for j in range(RING)]
        for name in names:
            self.sems[name] = ctx.enter_context(nc.semaphore(name))
            self.cnt[name] = 0

    def emit(self, eng, fn, waits=None, inc=None, n=1, reserved=False):
        real = []
        for s, v in (waits or []):
            if v is None or v <= 0:
                continue
            key = (eng, s)
            if v > self.waited.get(key, 0):
                self.waited[key] = v
                real.append((self.sems[s], v))
        cap = None
        if inc and not reserved:
            self.cnt[inc] += n
            cap = self.cnt[inc]
        sem = self.sems[inc] if inc else None

        def closure(h, real=real, fn=fn, sem=sem, n=n):
            for s, v in real:
                h.wait_ge(s, v)
            ins = fn(h)
            if sem is not None:
                ins.then_inc(sem, n)
        self.ops[eng].append(closure)
        return cap

    def finish(self, block):
        for eng in ENGINES:
            lst = self.ops[eng]
            if not lst:
                continue
            def body(h, lst=lst):
                for fn in lst:
                    fn(h)
            getattr(block, eng)(body)


def build():
    nc = bass.Bass("TRN2", target_bir_lowering=False, debug=False,
                   num_swdge_queues=4)
    dt = nc.dram_tensor
    x_h = dt("x", [D, T], F32, kind="ExternalInput").ap()
    wq_h = dt("wq", [D, D], F32, kind="ExternalInput").ap()
    wk_h = dt("wk", [D, D], F32, kind="ExternalInput").ap()
    wv_h = dt("wv", [D, D], F32, kind="ExternalInput").ap()
    wo_h = dt("wo", [D, D], F32, kind="ExternalInput").ap()
    bq_h = dt("bq", [128, HC], F32, kind="ExternalInput").ap()
    bo_h = dt("bo", [128, HC], F32, kind="ExternalInput").ap()
    gam_h = dt("gam", [128, HC], F32, kind="ExternalInput").ap()
    bet_h = dt("bet", [128, HC], F32, kind="ExternalInput").ap()
    bkr_h = dt("bkr", [1, D], F32, kind="ExternalInput").ap()
    bvr_h = dt("bvr", [1, D], F32, kind="ExternalInput").ap()
    gwt_h = dt("gwt", [D, 8], F32, kind="ExternalInput").ap()
    ident_h = dt("ident", [128, 128], F32, kind="ExternalInput").ap()
    sel_h = dt("sel", [8, D], F32, kind="ExternalInput").ap()
    wg_h = dt("wg", [8, D, IE], F32, kind="ExternalInput").ap()
    wu_h = dt("wu", [8, D, IE], F32, kind="ExternalInput").ap()
    wd_h = dt("wd", [8, IE, D], F32, kind="ExternalInput").ap()
    swg_h = dt("swg", [D, 2 * IE], F32, kind="ExternalInput").ap()
    swu_h = dt("swu", [D, 2 * IE], F32, kind="ExternalInput").ap()
    swd_h = dt("swd", [2 * IE, D], F32, kind="ExternalInput").ap()
    y_h = dt("y", [D, T], F32, kind="ExternalOutput").ap()
    cc_in = dt("cc_in", [128, 512], BF16).ap()
    cc_out = dt("cc_out", [128, 512], BF16).ap()

    groups = [[0, 1, 2, 3], [4, 5, 6, 7]]

    with contextlib.ExitStack() as ctx:
        E = ctx.enter_context
        S = E(nc.sbuf_tensor("S", [128, HC, T], F32))
        X16 = E(nc.sbuf_tensor("X16", [128, HC, T], BF16))
        bqs = E(nc.sbuf_tensor("bqs", [128, HC], F32))
        bos = E(nc.sbuf_tensor("bos", [128, HC], F32))
        gams = E(nc.sbuf_tensor("gams", [128, HC], F32))
        bets = E(nc.sbuf_tensor("bets", [128, HC], F32))
        gwts = E(nc.sbuf_tensor("gwts", [128, HC, 8], F32))
        idents = E(nc.sbuf_tensor("idents", [128, 128], F32))
        ones_c = E(nc.sbuf_tensor("ones_c", [128, 1], F32))
        ones_r = E(nc.sbuf_tensor("ones_r", [1, 128], F32))
        ones_c16 = E(nc.sbuf_tensor("ones_c16", [128, 1], BF16))
        ones_r16 = E(nc.sbuf_tensor("ones_r16", [1, 128], BF16))
        KV16 = E(nc.sbuf_tensor("KV16", [128, 512], BF16))
        KVs = E(nc.sbuf_tensor("KVs", [128, 512], BF16))

        bld = B(nc, ctx)
        blockh = E(nc.Block())

        xsems = ["LX", "LXB", "LXC", "LS"]
        x_lds = []
        TQ = T // 4
        for q in range(4):
            x_lds.append(bld.emit("sync", lambda h, q=q: h.dma_start(
                out=S[:, :, q * TQ:(q + 1) * TQ],
                in_=x_h.rearrange("(c p) t -> p c t", p=128)[:, :, q * TQ:(q + 1) * TQ]),
                inc=xsems[q], n=16))
        x_ld = x_lds[3]

        def g_dma(dst, src, waits=None, sem="LG"):
            return bld.emit("gpsimd",
                            lambda h, d=dst, s=src: h.dma_start(out=d, in_=s),
                            waits=waits, inc=sem, n=16)

        g_dma(bqs[:, :], bq_h, sem="LMISC")
        g_dma(bos[:, :], bo_h, sem="LMISC")
        g_dma(gams[:, :], gam_h, sem="LMISC")
        g_dma(bets[:, :], bet_h, sem="LMISC")
        g_dma(gwts[:, :, :], gwt_h.rearrange("(c p) n -> p c n", p=128), sem="LMISC")
        misc_ld = g_dma(idents[:, :], ident_h, sem="LMISC")

        ones_ev = bld.emit("vector", lambda h: h.memset(ones_c[:, :], 1.0), inc="VV")
        ones_ev = bld.emit("vector", lambda h: h.memset(ones_r[:, :], 1.0), inc="VV")
        ones_ev = bld.emit("vector", lambda h: h.memset(ones_c16[:, :], 1.0), inc="VV")
        ones_ev = bld.emit("vector", lambda h: h.memset(ones_r16[:, :], 1.0), inc="VV")
        x16_q = []
        for q in range(4):
            last = None
            for c_ in range(HC):
                last = bld.emit("vector",
                    lambda h, c_=c_, q=q: h.tensor_copy(
                        X16[:, c_, q * TQ:(q + 1) * TQ],
                        S[:, c_, q * TQ:(q + 1) * TQ]),
                    waits=[(xsems[q], x_lds[q])], inc="VV")
            x16_q.append(last)
        x16_ld = x16_q[3]
        x16_h1 = x16_q[1]

        with contextlib.ExitStack() as actx:
            AE = actx.enter_context
            wq16 = AE(nc.sbuf_tensor("wq16", [128, HC, D], BF16))
            wo16 = AE(nc.sbuf_tensor("wo16", [128, HC, D], BF16))

            kvr_ld, endA, wq_ld, wo_ld = _phase_a(
                nc, bld, actx, S, X16, wk_h, wv_h,
                bkr_h, bvr_h, ones_r, ones_ev, x16_q, g_dma,
                cc_in, cc_out, groups, KVs, KV16,
                wq16, wq_h, wo16, wo_h)

            endC = _phase_c(nc, bld, actx, S, X16, wq16, wo16, KV16,
                            bqs, bos, gams, bets, ones_c16, ones_r16,
                            wq_ld, wo_ld, kvr_ld, endA, x_ld, misc_ld)

        if STAGE == 1:
            fin = bld.emit("sync", lambda h: h.dma_start(
                out=y_h.rearrange("(c p) t -> p c t", p=128), in_=S[:, :, :]),
                waits=[("VV", endC["VV"])], inc="LS", n=16)
            bld.emit("sync", lambda h: h.nop(), waits=[("LS", fin)])
        else:
            _ffn(nc, bld, ctx, S, X16, gwts, idents, ones_c16, ones_r16,
                 wg_h, wu_h, wd_h, swg_h, swu_h, swd_h, sel_h, y_h, endC)

        bld.finish(blockh)
    return nc


def _phase_a(nc, bld, actx, S, X16, wk_h, wv_h, bkr_h, bvr_h, ones_r, ones_ev,
             x16_q, g_dma, cc_in, cc_out, groups, KVs, KV16,
             wq16, wq_h, wo16, wo_h):
    with contextlib.ExitStack() as pa:
        P = pa.enter_context
        wk16 = P(nc.sbuf_tensor("wk16", [128, HC, D], BF16))
        wv16 = P(nc.sbuf_tensor("wv16", [128, HC, D], BF16))
        PRE_LOADS = True
        bkrep = P(nc.sbuf_tensor("bkrep", [128, D], F32))
        bvrep = P(nc.sbuf_tensor("bvrep", [128, D], F32))
        kf1 = P(nc.sbuf_tensor("kf1", [128, 2, D], F32))
        kf2 = P(nc.sbuf_tensor("kf2", [128, 2, D], F32))
        kf3 = P(nc.sbuf_tensor("kf3", [128, 2, D], F32))
        ks1 = P(nc.sbuf_tensor("ks1", [128, 2, 16], F32))
        ks4 = P(nc.sbuf_tensor("ks4", [128, 2, 16], F32))
        kn16 = P(nc.sbuf_tensor("kn16", [128, 2, D], BF16))
        vn16 = P(nc.sbuf_tensor("vn16", [128, 2, D], BF16))
        kp = P(nc.psum_tensor("kp", [128, 2, D], F32))
        vp = P(nc.psum_tensor("vp", [128, D], F32))
        KVp = P(nc.psum_tensor("KVp", [128, 512], F32))

        g_dma(kf1[0:1, 0, :], bkr_h, sem="LBK")
        bkv_ld = g_dma(kf1[0:1, 1, :], bvr_h, sem="LBK")
        wk_ld1 = g_dma(wk16[:, 0:4, :],
                       wk_h.rearrange("(c p) n -> p c n", p=128)[:, 0:4, :], sem="LWKH")
        wv_ld1 = g_dma(wv16[:, 0:4, :],
                       wv_h.rearrange("(c p) n -> p c n", p=128)[:, 0:4, :], sem="LWVH")
        wk_ld = g_dma(wk16[:, 4:8, :],
                      wk_h.rearrange("(c p) n -> p c n", p=128)[:, 4:8, :], sem="LWK")
        wv_ld = g_dma(wv16[:, 4:8, :],
                      wv_h.rearrange("(c p) n -> p c n", p=128)[:, 4:8, :], sem="LWV")
        wq_ld = g_dma(wq16[:, :, :], wq_h.rearrange("(c p) n -> p c n", p=128), sem="LWQ")
        wo_ld = g_dma(wo16[:, :, :], wo_h.rearrange("(c p) n -> p c n", p=128), sem="LWO")

        def rep_mm(h, j):
            for half in range(2):
                ins = h.matmul(kp[:, j, half * 512:(half + 1) * 512],
                               ones_r[:, :], kf1[0:1, j, half * 512:(half + 1) * 512],
                               start=True, stop=True)
            return ins
        rb_ev = bld.emit("tensor", lambda h: rep_mm(h, 0),
                         waits=[("LBK", bkv_ld), ("VV", ones_ev)], inc="TT")
        rb_ev2 = bld.emit("tensor", lambda h: rep_mm(h, 1), inc="TT")
        bkrep_ev = bld.emit("vector", lambda h: h.tensor_copy(bkrep[:, :], kp[:, 0, :]),
                            waits=[("TT", rb_ev)], inc="VV")
        bvrep_ev = bld.emit("vector", lambda h: h.tensor_copy(bvrep[:, :], kp[:, 1, :]),
                            waits=[("TT", rb_ev2)], inc="VV")

        kb_ev = {}; vb_ev = {}; kn_ev = {}; kv_ev = {}; sqA_ev = {}
        kv_defer = {}
        for t in range(NT):
            ts0, ts1 = t * 128, (t + 1) * 128

            def kv_proj(h, t=t, ts0=ts0, ts1=ts1, hlo=0, hhi=4):
                for hh in range(hlo, hhi):
                    for half in range(2):
                        h.matmul(kp[:, t % 2, half * 512:(half + 1) * 512],
                                 X16[:, hh, ts0:ts1],
                                 wk16[:, hh, half * 512:(half + 1) * 512],
                                 start=(hh == 0), stop=(hh == HC - 1))
                    for half in range(2):
                        ins = h.matmul(vp[:, half * 512:(half + 1) * 512],
                                       X16[:, hh, ts0:ts1],
                                       wv16[:, hh, half * 512:(half + 1) * 512],
                                       start=(hh == 0), stop=(hh == HC - 1))
                return ins
            w = [("VV", x16_q[t * 4 // NT]),
                 ("LWKH", wk_ld1), ("LWVH", wv_ld1), ("VV", bvrep_ev)]
            if t >= 2:
                w.append(("VV", kb_ev[t - 2]))
            if t >= 1:
                w.append(("VV", vb_ev[t - 1]))
            bld.emit("tensor", kv_proj, waits=w, inc="TT")
            k_ev = bld.emit("tensor",
                            lambda h, kv_proj=kv_proj: kv_proj(h, hlo=4, hhi=8),
                            waits=[("LWK", wk_ld), ("LWV", wv_ld)], inc="TT")
            v_ev = k_ev
            if (t - 2) in kv_defer:
                fn_, tprev = kv_defer.pop(t - 2)
                kv_ev[tprev] = bld.emit("tensor", fn_,
                    waits=[("VV", kn_ev[tprev]), ("VV", vb_ev[tprev])], inc="TT")

            tb = t % 2
            w = [("TT", v_ev)]
            if t >= 2:
                w.append(("TT", kv_ev[t - 2]))
            vb_ev[t] = bld.emit("vector",
                lambda h, tb=tb: h.tensor_tensor(vn16[:, tb, :], vp[:, :], bvrep[:, :], ALU.add),
                waits=w, inc="VV")

            w = [("TT", k_ev)]
            if t >= 2:
                w.append(("SS", sqA_ev[t - 2]))
            kb_ev[t] = bld.emit("vector",
                lambda h, tb=tb: h.tensor_tensor(kf1[:, tb, :], kp[:, tb, :], bkrep[:, :], ALU.add),
                waits=w, inc="VV")
            r_ev = bld.emit("scalar",
                lambda h, tb=tb: h.activation(kf2[:, tb, :], kf1[:, tb, :], AF.Relu),
                waits=[("VV", kb_ev[t])], inc="SS")
            sb_ev = bld.emit("vector",
                lambda h, tb=tb: h.tensor_tensor(kf3[:, tb, :], kf1[:, tb, :], kf2[:, tb, :], ALU.subtract),
                waits=[("SS", r_ev)], inc="VV")
            e_ev = bld.emit("scalar",
                lambda h, tb=tb: h.activation(kf1[:, tb, :], kf3[:, tb, :], AF.Exp),
                waits=[("VV", sb_ev)], inc="SS")
            k1_ev = bld.emit("vector",
                lambda h, tb=tb: h.tensor_tensor(kf3[:, tb, :], kf1[:, tb, :], kf2[:, tb, :], ALU.add),
                waits=[("SS", e_ev)], inc="VV")
            sqA_ev[t] = bld.emit("scalar",
                lambda h, tb=tb: h.activation(kf1[:, tb, :], kf3[:, tb, :], AF.Square),
                waits=[("VV", k1_ev)], inc="SS")
            seg_ev = bld.emit("vector",
                lambda h, tb=tb: h.tensor_reduce(ks1[:, tb, :],
                                          kf1[:, tb, :].rearrange("p (s d) -> p s d", s=16),
                                          AXL.X, ALU.add),
                waits=[("SS", sqA_ev[t])], inc="VV")
            epsA = bld.emit("vector",
                lambda h, tb=tb: h.tensor_scalar(ks4[:, tb, :], ks1[:, tb, :],
                                                 EPS_IN, None, ALU.add),
                waits=[("VV", seg_ev)], inc="VV")
            ln_ev = bld.emit("scalar",
                lambda h, tb=tb: h.activation(ks4[:, tb, :], ks4[:, tb, :], AF.Ln),
                waits=[("VV", epsA)], inc="SS")
            rst_ev = bld.emit("scalar",
                lambda h, tb=tb: h.activation(ks4[:, tb, :], ks4[:, tb, :], AF.Exp, scale=-0.5),
                waits=[("SS", ln_ev)], inc="SS")
            w = [("SS", rst_ev)]
            if t >= 2:
                w.append(("TT", kv_ev[t - 2]))
            kn_ev[t] = bld.emit("vector",
                lambda h, tb=tb: h.tensor_tensor(
                    kn16[:, tb, :].rearrange("p (s d) -> p s d", s=16),
                    kf3[:, tb, :].rearrange("p (s d) -> p s d", s=16),
                    ks4[:, tb, :].unsqueeze(2).broadcast_to([128, 16, 64]),
                    ALU.mult),
                waits=w, inc="VV")

            def kv_mms(h, t=t):
                for hd in range(NH):
                    po = 64 * (hd % 2)
                    fo = 64 * (hd // 2)
                    ins = h.matmul(KVp[po:po + 64, fo:fo + 64],
                                   kn16[:, t % 2, hd * 64:(hd + 1) * 64],
                                   vn16[:, t % 2, hd * 64:(hd + 1) * 64],
                                   start=(t == 0 and hd < 2), stop=(t == NT - 1),
                                   tile_position=(0, po),
                                   skip_group_check=True)
                return ins
            kv_defer[t] = (kv_mms, t)
        for tprev in sorted(kv_defer):
            fn_, tp = kv_defer.pop(tprev)
            kv_ev[tp] = bld.emit("tensor", fn_,
                waits=[("VV", kn_ev[tp]), ("VV", vb_ev[tp])], inc="TT")
        kvs_ev = bld.emit("vector", lambda h: h.tensor_copy(KVs[:, :], KVp[:, :]),
                          waits=[("TT", kv_ev[NT - 1])], inc="VV")
        endA = dict(bld.cnt)
        st_ev = bld.emit("gpsimd", lambda h: h.dma_start(out=cc_in, in_=KVs[:, :]),
                         waits=[("VV", kvs_ev)], inc="LG", n=16)

        cc_ev = bld.emit("gpsimd",
            lambda h: h.collective_compute("AllReduce", ALU.add,
                                           replica_groups=groups,
                                           ins=[cc_in], outs=[cc_out]),
            waits=[("LG", st_ev)], inc="CC")
        kv16_ld = bld.emit("gpsimd", lambda h: h.dma_start(out=KV16[:, :], in_=cc_out),
                           waits=[("CC", cc_ev)], inc="LG", n=16)
        return kv16_ld, endA, wq_ld, wo_ld


def _phase_c(nc, bld, actx, S, X16, wq16, wo16, KV16, bqs, bos, gams, bets,
             ones_c, ones_r, wq_ld, wo_ld, kvr_ld, endA, x_ld, misc_final):
    with contextlib.ExitStack() as pc:
        P = pc.enter_context
        Q16 = P(nc.sbuf_tensor("Q16", [128, 8, CW], BF16))
        A16 = P(nc.sbuf_tensor("A16", [128, HC, CW], BF16))
        Ybuf = P(nc.sbuf_tensor("Ybuf", [128, 2, HC, CW], F32))
        qf1 = P(nc.sbuf_tensor("qf1", [128, 2, CW], F32))
        qf2 = P(nc.sbuf_tensor("qf2", [128, 2, CW], F32))
        qf3 = P(nc.sbuf_tensor("qf3", [128, 2, CW], F32))
        ysc = P(nc.sbuf_tensor("ysc", [128, 2, CW], F32))
        sqb = P(nc.sbuf_tensor("sqb", [128, 2, CW], BF16))
        y16 = P(nc.sbuf_tensor("y16", [128, 2, CW], BF16))
        rows = P(nc.sbuf_tensor("rows", [128, 3, CW], F32))
        rows16 = P(nc.sbuf_tensor("rows16", [1, 2, CW], BF16))
        qp = P(nc.psum_tensor("qp", [128, 2, CW], F32))
        attnp = P(nc.psum_tensor("attnp", [128, CW], F32))
        wop = P(nc.psum_tensor("wop", [128, 2, CW], F32))
        stats = P(nc.psum_tensor("stats", [1, 2, CW], F32))
        rep = P(nc.psum_tensor("rep", [128, CW], F32))

        barrA = [("TT", endA["TT"]), ("VV", endA["VV"]), ("SS", endA["SS"]),
                 ("LWK", endA["LWK"]), ("LWV", endA["LWV"]),
                 ("LWKH", endA["LWKH"]), ("LWVH", endA["LWVH"]),
                 ("LBK", endA["LBK"]),
                 ("LG", endA["LG"]), ("LS", x_ld)]

        def ln_rows(eps, stat_ev):
            ev = bld.emit("vector",
                lambda h: h.tensor_scalar(rows[0:1, 0, :], stats[0:1, 0, :],
                                          -1.0 / D, None, ALU.mult),
                waits=[("TT", stat_ev)], inc="VV")
            stats_free = bld.emit("vector",
                lambda h: h.tensor_scalar(rows[0:1, 1, :], stats[0:1, 1, :],
                                          1.0 / D, None, ALU.mult),
                inc="VV")
            r0c = bld.emit("vector",
                lambda h: h.tensor_copy(rows16[0:1, 0, :], rows[0:1, 0, :]),
                waits=[("VV", stats_free)], inc="VV")
            ev = bld.emit("vector",
                lambda h: h.tensor_tensor(rows[0:1, 2, :], rows[0:1, 0, :],
                                          rows[0:1, 0, :], ALU.mult),
                waits=[("VV", stats_free)], inc="VV")
            ev = bld.emit("vector",
                lambda h: h.tensor_tensor(rows[0:1, 1, :], rows[0:1, 1, :],
                                          rows[0:1, 2, :], ALU.subtract),
                waits=[("VV", ev)], inc="VV")
            ev = bld.emit("vector",
                lambda h, eps=eps: h.tensor_scalar(rows[0:1, 2, :], rows[0:1, 1, :],
                                                   eps, None, ALU.add),
                waits=[("VV", ev)], inc="VV")
            ev = bld.emit("scalar",
                lambda h: h.activation(rows[0:1, 2, :], rows[0:1, 2, :], AF.Ln),
                waits=[("VV", ev)], inc="SS")
            rstd_ev = bld.emit("scalar",
                lambda h: h.activation(rows[0:1, 2, :], rows[0:1, 2, :],
                                       AF.Exp, scale=-0.5),
                waits=[("SS", ev)], inc="SS")
            r2c = bld.emit("vector",
                lambda h: h.tensor_copy(rows16[0:1, 1, :], rows[0:1, 2, :]),
                waits=[("SS", rstd_ev)], inc="VV")
            return stats_free, r0c, r2c

        st = {"stats_free": None, "rep_free": None, "prev_stat2": None,
              "last_x16": None}
        qp_guard = {}
        q1_ev = {}
        attn_ev = {}
        atev_ev = {}
        y_evs = {}
        ln_y_free = {}

        def emit_qaw(u):
            ub = u % 2
            us0, us1 = u * CW, (u + 1) * CW
            for c in range(HC):
                k = u * HC + c

                def q_mms(h, c=c, us0=us0, us1=us1):
                    for hh in range(HC):
                        ins = h.matmul(qp[:, c % 2, :],
                                       wq16[:, hh, c * 128:(c + 1) * 128],
                                       X16[:, hh, us0:us1],
                                       start=(hh == 0), stop=(hh == HC - 1))
                    return ins
                w = list(barrA) + [("LWQ", wq_ld)]
                if k >= 2:
                    w.append(("VV", qp_guard[k - 2]))
                qpe = bld.emit("tensor", q_mms, waits=w, inc="TT")

                w = [("TT", qpe), ("LMISC", misc_final)]
                if k >= 2:
                    w.append(("VV", q1_ev[k - 2]))
                rq = bld.emit("scalar",
                    lambda h, c=c: h.activation(qf1[:, c % 2, :], qp[:, c % 2, :],
                                                AF.Relu, bias=bqs[:, c:c + 1]),
                    waits=w, inc="SS")
                mq = bld.emit("vector",
                    lambda h, c=c: h.scalar_tensor_tensor(
                        qf2[:, c % 2, :], qp[:, c % 2, :], bqs[:, c:c + 1],
                        qf1[:, c % 2, :], ALU.add, ALU.subtract),
                    waits=[("SS", rq), ("LMISC", misc_final)], inc="VV")
                qp_guard[k] = mq
                eq = bld.emit("scalar",
                    lambda h, c=c: h.activation(qf3[:, c % 2, :], qf2[:, c % 2, :], AF.Exp),
                    waits=[("VV", mq)], inc="SS")
                w = [("SS", eq)]
                if k >= 8:
                    w.append(("TT", attn_ev[k - 8]))
                q1_ev[k] = bld.emit("vector",
                    lambda h, k=k, c=c: h.tensor_tensor(Q16[:, k % 8, :], qf1[:, c % 2, :],
                                                        qf3[:, c % 2, :], ALU.add),
                    waits=w, inc="VV")

                def emit_attn(kk):
                    cc_ = kk % HC

                    def a_mms(h, kk=kk, cc_=cc_):
                        h.matmul(attnp[0:64, :], KV16[0:64, cc_ * 64:(cc_ + 1) * 64],
                                 Q16[0:64, kk % 8, :], start=True, stop=True)
                        return h.matmul(attnp[64:128, :], KV16[64:128, cc_ * 64:(cc_ + 1) * 64],
                                        Q16[64:128, kk % 8, :], start=True, stop=True,
                                        tile_position=(64, 64))
                    w = [("VV", q1_ev[kk]), ("LG", kvr_ld)]
                    if kk >= 1:
                        w.append(("SS", atev_ev[kk - 1]))
                    attn_ev[kk] = bld.emit("tensor", a_mms, waits=w, inc="TT")
                    atev_ev[kk] = bld.emit("scalar",
                        lambda h, cc_=cc_: h.activation(A16[:, cc_, :], attnp[:, :], AF.Copy),
                        waits=[("TT", attn_ev[kk])], inc="SS")
                yield

            for kk in range(u * HC, u * HC + HC):
                emit_attn(kk)
            yield

            y_ev = {}
            yb_guard = {}
            for o in range(HC):
                def wo_mms(h, o=o):
                    for cc2 in range(HC):
                        ins = h.matmul(wop[:, o % 2, :],
                                       wo16[:, cc2, o * 128:(o + 1) * 128],
                                       A16[:, cc2, :],
                                       start=(cc2 == 0), stop=(cc2 == HC - 1))
                    return ins
                w = [("SS", atev_ev[u * HC + HC - 1]), ("LWO", wo_ld)]
                if o >= 2:
                    w.append(("SS", yb_guard[o - 2]))
                woe = bld.emit("tensor", wo_mms, waits=w, inc="TT")
                w = [("TT", woe), ("LMISC", misc_final)]
                if o >= 2:
                    w.append(("VV", y_ev[o - 2]))
                if o < 2 and u >= 1:
                    w.append(("VV", y_evs[u - 1][HC - 2 + o]))
                yb_guard[o] = bld.emit("scalar",
                    lambda h, o=o: h.activation(ysc[:, o % 2, :], wop[:, o % 2, :],
                                                AF.Identity, bias=bos[:, o:o + 1]),
                    waits=w, inc="SS")
                w = [("SS", yb_guard[o])]
                if u >= 2:
                    w.append(("VV", ln_y_free[u - 2]))
                y_ev[o] = bld.emit("vector",
                    lambda h, o=o, ub=ub, us0=us0, us1=us1: h.tensor_tensor(
                        Ybuf[:, ub, o, :], ysc[:, o % 2, :], S[:, o, us0:us1], ALU.add),
                    waits=w, inc="VV")
                yield
            y_evs[u] = y_ev

        def emit_ln(u):
            ub = u % 2
            us0, us1 = u * CW, (u + 1) * CW
            y_ev = y_evs[u]
            stat_g = {}
            for c in range(HC):
                w = [("VV", y_ev[c])]
                if c >= 2:
                    w.append(("TT", stat_g[c - 2]))
                elif st["prev_stat2"] is not None:
                    w.append(("TT", st["prev_stat2"][HC - 2 + c]))
                sqe = bld.emit("scalar",
                    lambda h, c=c, ub=ub: h.activation(sqb[:, c % 2, :],
                                                       Ybuf[:, ub, c, :], AF.Square),
                    waits=w, inc="SS")
                y16e = bld.emit("gpsimd",
                    lambda h, c=c, ub=ub: h.tensor_copy(y16[:, c % 2, :],
                                                        Ybuf[:, ub, c, :]),
                    waits=w, inc="GG")

                def st_mms(h, c=c, ub=ub):
                    h.matmul(stats[0:1, 0, :], ones_c[:, :], y16[:, c % 2, :],
                             start=(c == 0), stop=(c == HC - 1))
                    return h.matmul(stats[0:1, 1, :], ones_c[:, :], sqb[:, c % 2, :],
                                    start=(c == 0), stop=(c == HC - 1))
                w = [("SS", sqe), ("GG", y16e), ("VV", y_ev[c])]
                if c == 0 and st["stats_free"] is not None:
                    w.append(("VV", st["stats_free"]))
                stat_g[c] = bld.emit("tensor", st_mms, waits=w, inc="TT")
                yield

            stats_free, r0c, r2c = ln_rows(EPS_IN, stat_g[HC - 1])

            w = [("VV", r0c)]
            if st["rep_free"] is not None:
                w.append(("VV", st["rep_free"]))
            rme = bld.emit("tensor",
                lambda h: h.matmul(rep[:, :], ones_r[:, :], rows16[0:1, 0, :],
                                   start=True, stop=True),
                waits=w, inc="TT")
            yield
            ev = None
            for c in range(HC):
                ev = bld.emit("vector",
                    lambda h, c=c, ub=ub: h.tensor_tensor(Ybuf[:, ub, c, :],
                                                          Ybuf[:, ub, c, :],
                                                          rep[:, :], ALU.add),
                    waits=[("TT", rme)], inc="VV")
            rre = bld.emit("tensor",
                lambda h: h.matmul(rep[:, :], ones_r[:, :], rows16[0:1, 1, :],
                                   start=True, stop=True),
                waits=[("VV", ev), ("VV", r2c)], inc="TT")
            yield
            for c in range(HC):
                ev = bld.emit("vector",
                    lambda h, c=c, ub=ub: h.tensor_tensor(Ybuf[:, ub, c, :],
                                                          Ybuf[:, ub, c, :],
                                                          rep[:, :], ALU.mult),
                    waits=[("TT", rre)], inc="VV")
            gb_ev = None
            for c in range(HC):
                gb_ev = bld.emit("scalar",
                    lambda h, c=c, ub=ub: h.activation(Ybuf[:, ub, c, :],
                                                       Ybuf[:, ub, c, :], AF.Identity,
                                                       bias=bets[:, c:c + 1],
                                                       scale=gams[:, c:c + 1]),
                    waits=[("VV", ev), ("LMISC", misc_final)], inc="SS")
            for c in range(HC):
                ev = bld.emit("vector",
                    lambda h, c=c, ub=ub, us0=us0, us1=us1: h.tensor_tensor(
                        Ybuf[:, ub, c, :], Ybuf[:, ub, c, :], S[:, c, us0:us1], ALU.add),
                    waits=[("SS", gb_ev)], inc="VV")
            w_done = ev

            stat2_g = {}
            for c in range(HC):
                w = [("VV", w_done)]
                if c >= 2:
                    w.append(("TT", stat2_g[c - 2]))
                sqe = bld.emit("scalar",
                    lambda h, c=c, ub=ub: h.activation(sqb[:, c % 2, :],
                                                       Ybuf[:, ub, c, :], AF.Square),
                    waits=w, inc="SS")
                y16e = bld.emit("gpsimd",
                    lambda h, c=c, ub=ub: h.tensor_copy(y16[:, c % 2, :],
                                                        Ybuf[:, ub, c, :]),
                    waits=w, inc="GG")

                def st2_mms(h, c=c, ub=ub):
                    h.matmul(stats[0:1, 0, :], ones_c[:, :], y16[:, c % 2, :],
                             start=(c == 0), stop=(c == HC - 1))
                    return h.matmul(stats[0:1, 1, :], ones_c[:, :], sqb[:, c % 2, :],
                                    start=(c == 0), stop=(c == HC - 1))
                w = [("SS", sqe), ("GG", y16e), ("VV", w_done)]
                if c == 0:
                    w.append(("VV", stats_free))
                stat2_g[c] = bld.emit("tensor", st2_mms, waits=w, inc="TT")
                yield
            st["prev_stat2"] = stat2_g

            stats_free, r0c, r2c = ln_rows(EPS_OUT, stat2_g[HC - 1])
            st["stats_free"] = stats_free

            rme2 = bld.emit("tensor",
                lambda h: h.matmul(rep[:, :], ones_r[:, :], rows16[0:1, 0, :],
                                   start=True, stop=True),
                waits=[("VV", r0c), ("VV", ev)], inc="TT")
            yield
            for c in range(HC):
                ev = bld.emit("vector",
                    lambda h, c=c, ub=ub: h.tensor_tensor(Ybuf[:, ub, c, :],
                                                          Ybuf[:, ub, c, :],
                                                          rep[:, :], ALU.add),
                    waits=[("TT", rme2)], inc="VV")
            rre2 = bld.emit("tensor",
                lambda h: h.matmul(rep[:, :], ones_r[:, :], rows16[0:1, 1, :],
                                   start=True, stop=True),
                waits=[("VV", ev), ("VV", r2c)], inc="TT")
            yield
            for c in range(HC):
                ev = bld.emit("vector",
                    lambda h, c=c, ub=ub, us0=us0, us1=us1: h.tensor_tensor(
                        S[:, c, us0:us1], Ybuf[:, ub, c, :], rep[:, :], ALU.mult),
                    waits=[("TT", rre2)], inc="VV")
            st["rep_free"] = ev
            ln_y_free[u] = ev
            xc_ev = None
            for c in range(HC):
                xc_ev = bld.emit("scalar",
                    lambda h, c=c, us0=us0, us1=us1: h.copy(
                        X16[:, c, us0:us1], S[:, c, us0:us1]),
                    waits=[("VV", ev)], inc="SS")
            st["last_x16"] = xc_ev

        pending = []

        def pump(k):
            while k > 0 and pending:
                try:
                    next(pending[0])
                    k -= 1
                except StopIteration:
                    pending.pop(0)

        SCHED = [0, 1, 1, 1, 1, 1, 1, 1,
                 1,
                 1, 1, 0, 0, 0, 2, 2, 2]
        for u in range(NTC):
            qg = emit_qaw(u)
            pi = 0
            for _ in qg:
                if pi == 8:
                    while len(pending) > 1:
                        try:
                            next(pending[0])
                        except StopIteration:
                            pending.pop(0)
                pump(SCHED[pi] if pi < len(SCHED) else 1)
                pi += 1
            pending.append(emit_ln(u))
        while pending:
            try:
                next(pending[0])
            except StopIteration:
                pending.pop(0)
        return dict(bld.cnt)


def _ffn(nc, bld, ctx, S, X16, gwts, idents, ones_c, ones_r,
         wg_h, wu_h, wd_h, swg_h, swu_h, swd_h, sel_h, y_h, endC):
    E = ctx.enter_context
    ring = E(nc.sbuf_tensor("ring", [128, RING, 3, 1024], BF16))
    act = E(nc.sbuf_tensor("act", [128, 2, SW], BF16))
    actm = E(nc.sbuf_tensor("actm", [128, 2, SW], BF16))
    s1t = E(nc.sbuf_tensor("s1t", [128, 2, SW], BF16))
    w8sb = E(nc.sbuf_tensor("w8sb", [8, T], BF16))
    sel8 = E(nc.sbuf_tensor("sel8", [8, D], F32))
    sel16 = E(nc.sbuf_tensor("sel16", [8, D], BF16))
    w_tm = E(nc.sbuf_tensor("w_tm", [128, NT, 8], F32))
    ysq = E(nc.sbuf_tensor("ysq", [128, 2, SW], BF16))
    y16F = E(nc.sbuf_tensor("y16F", [128, 2, SW], BF16))
    rowsF = E(nc.sbuf_tensor("rowsF", [128, 3, SW], F32))
    rowsF16 = E(nc.sbuf_tensor("rowsF16", [1, 2, SW], BF16))
    wbs = E(nc.sbuf_tensor("wbs", [128, 2, SW], F32))
    pb1 = E(nc.sbuf_tensor("pb1", [128, NT, 8], F32))
    pb2 = E(nc.sbuf_tensor("pb2", [128, NT, 8], F32))
    pb3 = E(nc.sbuf_tensor("pb3", [128, NT, 8], F32))
    rt = E(nc.sbuf_tensor("rt", [128, NT, 6], F32))
    pairp = E(nc.psum_tensor("pairp", [128, 2, 512], F32))
    dnp = E(nc.psum_tensor("dnp", [128, HC, SW], F32))
    mfp = E(nc.psum_tensor("mfp", [128, 512], F32))
    gatep = E(nc.psum_tensor("gatep", [128, 512], F32))

    barrC = [("TT", endC["TT"]), ("VV", endC["VV"]), ("SS", endC["SS"]),
             ("LWQ", endC["LWQ"]), ("LWO", endC["LWO"]), ("LMISC", endC["LMISC"]),
             ("LG", endC["LG"]), ("LS", endC["LS"])]

    sel_ld = bld.emit("gpsimd", lambda h: h.dma_start(out=sel8[:, :], in_=sel_h),
                      waits=barrC, inc="LSEL", n=16)
    sel16_ev = bld.emit("vector", lambda h: h.tensor_copy(sel16[:, :], sel8[:, :]),
                        waits=[("LSEL", sel_ld)], inc="VV")

    w8_prev = None
    for t in range(NT):
        ts0, ts1 = t * 128, (t + 1) * 128

        def g_mms(h, ts0=ts0, ts1=ts1):
            for hh in range(HC):
                ins = h.matmul(gatep[:, 0:8], S[:, hh, ts0:ts1], gwts[:, hh, :],
                               start=(hh == 0), stop=(hh == HC - 1))
            return ins
        w = list(barrC)
        if w8_prev is not None:
            w.append(("VV", w8_prev))
        g_ev = bld.emit("tensor", g_mms, waits=w, inc="TT")
        lg_ev = bld.emit("vector",
            lambda h, t=t: h.tensor_copy(w_tm[:, t, :], gatep[:, 0:8]),
            waits=[("TT", g_ev)], inc="VV")

        ev = bld.emit("vector",
            lambda h, t=t: h.reduce_max(rt[:, t, 0:1], w_tm[:, t, :], AXL.X),
            waits=[("VV", lg_ev)], inc="VV")
        ev = bld.emit("vector",
            lambda h, t=t: h.tensor_scalar(rt[:, t, 1:2], rt[:, t, 0:1], -1.0, None, ALU.mult),
            waits=[("VV", ev)], inc="VV")
        pexp = bld.emit("scalar",
            lambda h, t=t: h.activation(pb1[:, t, :], w_tm[:, t, :], AF.Exp, bias=rt[:, t, 1:2]),
            waits=[("VV", ev)], inc="SS")
        zs = bld.emit("vector",
            lambda h, t=t: h.reduce_sum(rt[:, t, 2:3], pb1[:, t, :], AXL.X),
            waits=[("SS", pexp)], inc="VV")
        ev = bld.emit("vector",
            lambda h, t=t: h.reduce_max(rt[:, t, 3:4], pb1[:, t, :], AXL.X),
            waits=[("VV", zs)], inc="VV")
        ev = bld.emit("vector",
            lambda h, t=t: h.tensor_scalar(pb2[:, t, :], pb1[:, t, :], rt[:, t, 3:4], None, ALU.is_ge),
            waits=[("VV", ev)], inc="VV")
        ev = bld.emit("vector",
            lambda h, t=t: h.scalar_tensor_tensor(pb3[:, t, :], pb2[:, t, :], -2.0, pb1[:, t, :],
                                             ALU.mult, ALU.add),
            waits=[("VV", ev)], inc="VV")
        ev = bld.emit("vector",
            lambda h, t=t: h.reduce_max(rt[:, t, 4:5], pb3[:, t, :], AXL.X),
            waits=[("VV", ev)], inc="VV")
        ev = bld.emit("vector",
            lambda h, t=t: h.tensor_scalar(pb2[:, t, :], pb1[:, t, :], rt[:, t, 4:5], None, ALU.is_ge),
            waits=[("VV", ev)], inc="VV")
        ev = bld.emit("vector",
            lambda h, t=t: h.tensor_tensor(pb3[:, t, :], pb1[:, t, :], pb2[:, t, :], ALU.mult),
            waits=[("VV", ev)], inc="VV")
        ev = bld.emit("vector",
            lambda h, t=t: h.reduce_sum(rt[:, t, 5:6], pb3[:, t, :], AXL.X),
            waits=[("VV", ev)], inc="VV")
        ev = bld.emit("vector",
            lambda h, t=t: h.scalar_tensor_tensor(rt[:, t, 0:1], rt[:, t, 2:3], 1e-20, rt[:, t, 5:6],
                                             ALU.mult, ALU.add),
            waits=[("VV", ev)], inc="VV")
        ev = bld.emit("vector",
            lambda h, t=t: h.reciprocal(rt[:, t, 1:2], rt[:, t, 0:1]),
            waits=[("VV", ev)], inc="VV")
        wt_ev = bld.emit("vector",
            lambda h, t=t: h.tensor_scalar(w_tm[:, t, :], pb3[:, t, :], rt[:, t, 1:2], None, ALU.mult),
            waits=[("VV", ev)], inc="VV")

        tr_ev = bld.emit("tensor",
            lambda h, t=t: h.transpose(gatep[0:8, 128:256], w_tm[:, t, :], idents[:, :]),
            waits=[("VV", wt_ev)], inc="TT")
        w8_prev = bld.emit("vector",
            lambda h, ts0=ts0, ts1=ts1: h.tensor_copy(w8sb[0:8, ts0:ts1],
                                                      gatep[0:8, 128:256]),
            waits=[("TT", tr_ev)], inc="VV")
    routing_done = w8_prev

    nsets = NU * IC
    ld_at = {}
    for s in range(nsets):
        ld_at[s] = 48 * (s // RING + 1)
    for j in range(RING):
        bld.cnt[f"LW{j}"] = 48 * ((nsets - 1 - j) // RING + 1)

    set_last_use = {}
    pair_s = {}
    pair_v = {}
    dn_evs = {}
    vy_ev = {}
    wb_ev = {}

    def emit_dn(u, w_, i, k, b, slot, vy_prev):
        ws0, ws1 = w_ * SW, (w_ + 1) * SW

        def dnf(h, i=i, b=b, slot=slot):
            for o in range(HC):
                ins = h.matmul(dnp[:, o, :], ring[:, slot, 2, o * 128:(o + 1) * 128],
                               act[:, b, :],
                               start=(i == 0 and o % 2 == 0),
                               stop=(i == IC - 1 and o % 2 == 1))
            return ins
        wts = [("VV", pair_v[k])]
        if i == 0 and vy_prev is not None:
            wts.append(("VV", vy_prev))
        if w_ == 0:
            s_ = u * IC + i
            wts.append((f"LW{s_ % RING}", ld_at[s_]))
        dn_evs[k] = bld.emit("tensor", dnf, waits=wts, inc="TT")
        if w_ == NSW - 1:
            set_last_use[u * IC + i] = dn_evs[k]

    kglob = 0
    vy_prev = None
    av_last = {}
    wbs_ev = {}
    for u in range(NU):
        for w_ in range(NSW):
            g = u * NSW + w_
            ws0, ws1 = w_ * SW, (w_ + 1) * SW
            if u < 8:
                wts = [("VV", routing_done), ("VV", sel16_ev)]
                if g >= 2 and (g - 2) in av_last:
                    wts.append(("VV", av_last[g - 2]))
                if (u, w_) != (0, 0):
                    pass
                wb_ev[(u, w_)] = bld.emit("tensor",
                    lambda h, u=u, ws0=ws0, ws1=ws1: h.matmul(
                        mfp[:, 0:SW], sel16[:, u * 128:(u + 1) * 128],
                        w8sb[:, ws0:ws1], start=True, stop=True),
                    waits=wts, inc="TT")
                wbs_ev[g] = bld.emit("scalar",
                    lambda h, g=g: h.activation(wbs[:, g % 2, :], mfp[:, 0:SW], AF.Copy),
                    waits=[("TT", wb_ev[(u, w_)])] +
                          ([("VV", av_last[g - 2])] if (g - 2) in av_last else []),
                    inc="SS")

            pend = None
            for i in range(IC):
                k = kglob
                kglob += 1
                b = k % 2
                slot = (u * IC + i) % RING

                def p1(h, b=b, slot=slot, ws0=ws0, ws1=ws1):
                    for hh in range(HC):
                        h.matmul(pairp[:, b, 0:SW],
                                 ring[:, slot, 0, hh * 128:(hh + 1) * 128],
                                 X16[:, hh, ws0:ws1],
                                 start=(hh == 0), stop=False)
                    for hh in range(HC):
                        ins = h.matmul(pairp[:, b, SW:2 * SW],
                                       ring[:, slot, 1, hh * 128:(hh + 1) * 128],
                                       X16[:, hh, ws0:ws1],
                                       start=False, stop=(hh == HC - 1))
                    return ins
                wts = []
                if k >= 2:
                    wts += [("SS", pair_s[k - 2]), ("VV", pair_v[k - 2])]
                if w_ == 0:
                    s_ = u * IC + i
                    wts.append((f"LW{s_ % RING}", ld_at[s_]))
                if u == 0 and w_ == 0 and i == 0:
                    wts.append(("VV", routing_done))
                p1e = bld.emit("tensor", p1, waits=wts, inc="TT")
                if pend is not None:
                    emit_dn(*pend)

                wts = [("TT", p1e)]
                if k >= 2:
                    wts.append(("VV", pair_v[k - 2]))
                pair_s[k] = bld.emit("scalar",
                    lambda h, b=b: h.activation(s1t[:, b, :], pairp[:, b, 0:SW], AF.Silu),
                    waits=wts, inc="SS")
                if u < 8:
                    wts = [("SS", pair_s[k])]
                    if k >= 2:
                        wts.append(("TT", dn_evs[k - 2]))
                    mid = bld.emit("vector",
                        lambda h, b=b: h.tensor_tensor(actm[:, b, :], s1t[:, b, :],
                                                       pairp[:, b, SW:2 * SW], ALU.mult),
                        waits=wts, inc="VV")
                    pair_v[k] = bld.emit("vector",
                        lambda h, b=b, g=g: h.tensor_tensor(act[:, b, :], actm[:, b, :],
                                                            wbs[:, g % 2, :], ALU.mult),
                        waits=[("VV", mid), ("SS", wbs_ev[g])], inc="VV")
                else:
                    wts = [("SS", pair_s[k])]
                    if k >= 2:
                        wts.append(("TT", dn_evs[k - 2]))
                    pair_v[k] = bld.emit("vector",
                        lambda h, b=b: h.tensor_tensor(act[:, b, :], s1t[:, b, :],
                                                       pairp[:, b, SW:2 * SW], ALU.mult),
                        waits=wts, inc="VV")
                pend = (u, w_, i, k, b, slot, vy_prev)
            emit_dn(*pend)
            av_last[g] = pair_v[kglob - 1]

            ev = None
            for o in range(HC):
                wts = []
                if o == 0:
                    wts.append(("TT", dn_evs[kglob - 1]))
                ev = bld.emit("vector",
                    lambda h, o=o, ws0=ws0, ws1=ws1: h.tensor_tensor(
                        S[:, o, ws0:ws1], S[:, o, ws0:ws1], dnp[:, o, :], ALU.add),
                    waits=wts, inc="VV")
            vy_ev[(u, w_)] = ev
            vy_prev = ev

    def src_aps(u, i):
        if u < 8:
            return (wg_h[u].rearrange("(c p) n -> p c n", p=128)[:, :, i * 128:(i + 1) * 128],
                    wu_h[u].rearrange("(c p) n -> p c n", p=128)[:, :, i * 128:(i + 1) * 128],
                    wd_h[u][i * 128:(i + 1) * 128, :])
        off = (u - 8) * IE
        return (swg_h.rearrange("(c p) n -> p c n", p=128)[:, :, off + i * 128:off + (i + 1) * 128],
                swu_h.rearrange("(c p) n -> p c n", p=128)[:, :, off + i * 128:off + (i + 1) * 128],
                swd_h[off + i * 128:off + (i + 1) * 128, :])

    for s in range(nsets):
        u, i = s // IC, s % IC
        slot = s % RING
        gsrc, usrc, dsrc = src_aps(u, i)
        wts = list(barrC) if s < RING else [("TT", set_last_use[s - RING])]
        lw = f"LW{slot}"
        bld.emit("gpsimd", lambda h, slot=slot, gsrc=gsrc: h.dma_start(
            out=ring[:, slot, 0, :].rearrange("p (c n) -> p c n", c=HC), in_=gsrc),
            waits=wts, inc=lw, n=16, reserved=True)
        bld.emit("gpsimd", lambda h, slot=slot, usrc=usrc: h.dma_start(
            out=ring[:, slot, 1, :].rearrange("p (c n) -> p c n", c=HC), in_=usrc),
            inc=lw, n=16, reserved=True)
        bld.emit("gpsimd", lambda h, slot=slot, dsrc=dsrc: h.dma_start(
            out=ring[:, slot, 2, :], in_=dsrc), inc=lw, n=16, reserved=True)

    out_ld = None
    x2_done = None
    prev_fst_last = None
    prev_mfp_free = None
    for w_ in range(NSW):
        ws0, ws1 = w_ * SW, (w_ + 1) * SW
        fst_g = {}
        for c in range(HC):
            w = [("VV", vy_ev[(NU - 1, w_)])]
            if c >= 2:
                w.append(("TT", fst_g[c - 2]))
            elif prev_fst_last is not None:
                w.append(("TT", prev_fst_last))
            sqe = bld.emit("scalar",
                lambda h, c=c, ws0=ws0, ws1=ws1: h.activation(
                    ysq[:, c % 2, :], S[:, c, ws0:ws1], AF.Square),
                waits=w, inc="SS")
            y16e = bld.emit("scalar",
                lambda h, c=c, ws0=ws0, ws1=ws1: h.copy(
                    y16F[:, c % 2, :], S[:, c, ws0:ws1]),
                inc="SS")

            def fst_mms(h, c=c, ws0=ws0, ws1=ws1):
                h.matmul(mfp[0:1, 256:512], ones_c[:, :], y16F[:, c % 2, :],
                         start=(c == 0), stop=False)
                return h.matmul(mfp[0:1, 0:256], ones_c[:, :], ysq[:, c % 2, :],
                                start=False, stop=(c == HC - 1))
            w = [("SS", y16e), ("VV", vy_ev[(NU - 1, w_)])]
            if c == 0 and prev_mfp_free is not None:
                w.append(("VV", prev_mfp_free))
            fst_g[c] = bld.emit("tensor", fst_mms, waits=w, inc="TT")
        prev_fst_last = fst_g[HC - 1]

        ev = bld.emit("vector",
            lambda h: h.tensor_scalar(rowsF[0:1, 0, :], mfp[0:1, 256:512],
                                      -1.0 / D, None, ALU.mult),
            waits=[("TT", fst_g[HC - 1])], inc="VV")
        ev = bld.emit("vector",
            lambda h: h.tensor_scalar(rowsF[0:1, 1, :], mfp[0:1, 0:256],
                                      1.0 / D, None, ALU.mult),
            inc="VV")
        mfp_free = ev
        prev_mfp_free = ev
        rF0c = bld.emit("vector",
            lambda h: h.tensor_copy(rowsF16[0:1, 0, :], rowsF[0:1, 0, :]),
            waits=[("VV", mfp_free)], inc="VV")
        ev = bld.emit("vector",
            lambda h: h.tensor_tensor(rowsF[0:1, 2, :], rowsF[0:1, 0, :],
                                      rowsF[0:1, 0, :], ALU.mult),
            waits=[("VV", ev)], inc="VV")
        ev = bld.emit("vector",
            lambda h: h.tensor_tensor(rowsF[0:1, 1, :], rowsF[0:1, 1, :],
                                      rowsF[0:1, 2, :], ALU.subtract),
            waits=[("VV", ev)], inc="VV")
        ev = bld.emit("vector",
            lambda h: h.tensor_scalar(rowsF[0:1, 2, :], rowsF[0:1, 1, :],
                                      EPS_OUT, None, ALU.add),
            waits=[("VV", ev)], inc="VV")
        ev = bld.emit("scalar",
            lambda h: h.activation(rowsF[0:1, 2, :], rowsF[0:1, 2, :], AF.Ln),
            waits=[("VV", ev)], inc="SS")
        rste = bld.emit("scalar",
            lambda h: h.activation(rowsF[0:1, 2, :], rowsF[0:1, 2, :],
                                   AF.Exp, scale=-0.5),
            waits=[("SS", ev)], inc="SS")
        rF2c = bld.emit("vector",
            lambda h: h.tensor_copy(rowsF16[0:1, 1, :], rowsF[0:1, 2, :]),
            waits=[("SS", rste)], inc="VV")

        w = [("VV", rF0c)]
        if x2_done is not None:
            w.append(("VV", x2_done))
        rme = bld.emit("tensor",
            lambda h: h.matmul(gatep[:, 0:SW], ones_r[:, :], rowsF16[0:1, 0, :],
                               start=True, stop=False),
            waits=w, inc="TT")
        rre = bld.emit("tensor",
            lambda h: h.matmul(gatep[:, SW:2 * SW], ones_r[:, :], rowsF16[0:1, 1, :],
                               start=False, stop=True),
            waits=[("VV", rF2c)], inc="TT")
        ev = None
        for c in range(HC):
            ev = bld.emit("vector",
                lambda h, c=c, ws0=ws0, ws1=ws1: h.tensor_tensor(
                    S[:, c, ws0:ws1], S[:, c, ws0:ws1], gatep[:, 0:SW], ALU.add),
                waits=[("TT", rre)], inc="VV")
        for c in range(HC):
            ev = bld.emit("vector",
                lambda h, c=c, ws0=ws0, ws1=ws1: h.tensor_tensor(
                    S[:, c, ws0:ws1], S[:, c, ws0:ws1], gatep[:, SW:2 * SW], ALU.mult),
                waits=[("TT", rre), ("VV", ev)], inc="VV")
        x2_done = ev
        out_ld = bld.emit("sync", lambda h, ws0=ws0, ws1=ws1: h.dma_start(
            out=y_h.rearrange("(c p) t -> p c t", p=128)[:, :, ws0:ws1],
            in_=S[:, :, ws0:ws1]),
            waits=[("VV", x2_done)], inc="LS", n=16)

    bld.emit("sync", lambda h: h.nop(), waits=[("LS", out_ld)])


def kernel(L0_attn_x, L0_attn_Wq, L0_attn_bq, L0_attn_Wk, L0_attn_bk,
           L0_attn_Wv, L0_attn_bv, L0_attn_Wo, L0_attn_bo, L0_attn_gamma,
           L0_attn_beta, L0_ffn_gate_w, L0_ffn_wg, L0_ffn_wu, L0_ffn_wd,
           L0_ffn_swg, L0_ffn_swu, L0_ffn_swd):
    np32 = lambda a: np.ascontiguousarray(np.asarray(a, dtype=np.float32))
    x = np32(L0_attn_x).reshape(NCORES * T, D)
    fm = lambda b: np.ascontiguousarray(np32(b).reshape(HC, 128).T)
    sel = np.zeros((8, D), np.float32)
    for e in range(8):
        sel[e, e * 128:(e + 1) * 128] = 1.0
    common = {
        "wq": np32(L0_attn_Wq), "wk": np32(L0_attn_Wk),
        "wv": np32(L0_attn_Wv), "wo": np32(L0_attn_Wo),
        "bq": fm(L0_attn_bq), "bo": fm(L0_attn_bo),
        "gam": fm(L0_attn_gamma), "bet": fm(L0_attn_beta),
        "bkr": np32(L0_attn_bk).reshape(1, D), "bvr": np32(L0_attn_bv).reshape(1, D),
        "gwt": np.ascontiguousarray(np32(L0_ffn_gate_w).T),
        "ident": np.eye(128, dtype=np.float32),
        "sel": sel,
        "wg": np32(L0_ffn_wg), "wu": np32(L0_ffn_wu), "wd": np32(L0_ffn_wd),
        "swg": np32(L0_ffn_swg), "swu": np32(L0_ffn_swu), "swd": np32(L0_ffn_swd),
    }
    in_maps = []
    for c in range(NCORES):
        m = dict(common)
        m["x"] = np.ascontiguousarray(x[c * T:(c + 1) * T, :].T)
        in_maps.append(m)
    nc = build()
    res = run_bass_kernel_spmd(nc, in_maps, core_ids=list(range(NCORES)))
    outs = [np.asarray(res.results[c]["y"]).reshape(D, T).T for c in range(NCORES)]
    return np.concatenate(outs, axis=0).reshape(2, NCORES * T // 2, D).astype(np.float32)


# version 14
# speedup vs baseline: 1.0053x; 1.0053x over previous
import contextlib
import numpy as np

import concourse.bass as bass
import concourse.mybir as mybir
from concourse.bass_utils import run_bass_kernel_spmd

F32 = mybir.dt.float32
BF16 = mybir.dt.bfloat16
AF = mybir.ActivationFunctionType
ALU = mybir.AluOpType
AXL = mybir.AxisListType

NCORES = 8
D = 1024
HC = 8
T = 2048
NT = 16
NTC = 4
CW = 512
NSW = 8
SW = 256
NH = 16
IE = 1408
IC = 11
NU = 10
RING = 13
EPS_IN = 1e-6
EPS_OUT = 1e-5

ENGINES = ("tensor", "vector", "scalar", "gpsimd", "sync")
STAGE = 2


class B:
    def __init__(self, nc, ctx):
        self.nc = nc
        self.ops = {e: [] for e in ENGINES}
        self.sems = {}
        self.cnt = {}
        self.waited = {}
        names = ["TT", "VV", "SS", "GG", "LG", "LS", "CC", "LX", "LXB", "LXC",
                 "LWK", "LWV", "LWKH", "LWVH",
                 "LWQ", "LWO", "LBK", "LMISC", "LSEL"] + [f"LW{j}" for j in range(RING)]
        for name in names:
            self.sems[name] = ctx.enter_context(nc.semaphore(name))
            self.cnt[name] = 0

    def emit(self, eng, fn, waits=None, inc=None, n=1, reserved=False):
        real = []
        for s, v in (waits or []):
            if v is None or v <= 0:
                continue
            key = (eng, s)
            if v > self.waited.get(key, 0):
                self.waited[key] = v
                real.append((self.sems[s], v))
        cap = None
        if inc and not reserved:
            self.cnt[inc] += n
            cap = self.cnt[inc]
        sem = self.sems[inc] if inc else None

        def closure(h, real=real, fn=fn, sem=sem, n=n):
            for s, v in real:
                h.wait_ge(s, v)
            ins = fn(h)
            if sem is not None:
                ins.then_inc(sem, n)
        self.ops[eng].append(closure)
        return cap

    def finish(self, block):
        for eng in ENGINES:
            lst = self.ops[eng]
            if not lst:
                continue
            def body(h, lst=lst):
                for fn in lst:
                    fn(h)
            getattr(block, eng)(body)


def build():
    nc = bass.Bass("TRN2", target_bir_lowering=False, debug=False,
                   num_swdge_queues=4)
    dt = nc.dram_tensor
    x_h = dt("x", [D, T], F32, kind="ExternalInput").ap()
    wq_h = dt("wq", [D, D], F32, kind="ExternalInput").ap()
    wk_h = dt("wk", [D, D], F32, kind="ExternalInput").ap()
    wv_h = dt("wv", [D, D], F32, kind="ExternalInput").ap()
    wo_h = dt("wo", [D, D], F32, kind="ExternalInput").ap()
    bq_h = dt("bq", [128, HC], F32, kind="ExternalInput").ap()
    bo_h = dt("bo", [128, HC], F32, kind="ExternalInput").ap()
    gam_h = dt("gam", [128, HC], F32, kind="ExternalInput").ap()
    bet_h = dt("bet", [128, HC], F32, kind="ExternalInput").ap()
    bkr_h = dt("bkr", [1, D], F32, kind="ExternalInput").ap()
    bvr_h = dt("bvr", [1, D], F32, kind="ExternalInput").ap()
    gwt_h = dt("gwt", [D, 8], F32, kind="ExternalInput").ap()
    ident_h = dt("ident", [128, 128], F32, kind="ExternalInput").ap()
    sel_h = dt("sel", [8, D], F32, kind="ExternalInput").ap()
    wg_h = dt("wg", [8, D, IE], F32, kind="ExternalInput").ap()
    wu_h = dt("wu", [8, D, IE], F32, kind="ExternalInput").ap()
    wd_h = dt("wd", [8, IE, D], F32, kind="ExternalInput").ap()
    swg_h = dt("swg", [D, 2 * IE], F32, kind="ExternalInput").ap()
    swu_h = dt("swu", [D, 2 * IE], F32, kind="ExternalInput").ap()
    swd_h = dt("swd", [2 * IE, D], F32, kind="ExternalInput").ap()
    y_h = dt("y", [D, T], F32, kind="ExternalOutput").ap()
    cc_in = dt("cc_in", [128, 512], BF16).ap()
    cc_out = dt("cc_out", [128, 512], BF16).ap()

    groups = [[0, 1, 2, 3], [4, 5, 6, 7]]

    with contextlib.ExitStack() as ctx:
        E = ctx.enter_context
        S = E(nc.sbuf_tensor("S", [128, HC, T], F32))
        X16 = E(nc.sbuf_tensor("X16", [128, HC, T], BF16))
        bqs = E(nc.sbuf_tensor("bqs", [128, HC], F32))
        bos = E(nc.sbuf_tensor("bos", [128, HC], F32))
        gams = E(nc.sbuf_tensor("gams", [128, HC], F32))
        bets = E(nc.sbuf_tensor("bets", [128, HC], F32))
        gwts = E(nc.sbuf_tensor("gwts", [128, HC, 8], F32))
        idents = E(nc.sbuf_tensor("idents", [128, 128], F32))
        ones_c = E(nc.sbuf_tensor("ones_c", [128, 1], F32))
        ones_r = E(nc.sbuf_tensor("ones_r", [1, 128], F32))
        ones_c16 = E(nc.sbuf_tensor("ones_c16", [128, 1], BF16))
        ones_r16 = E(nc.sbuf_tensor("ones_r16", [1, 128], BF16))
        KV16 = E(nc.sbuf_tensor("KV16", [128, 512], BF16))
        KVs = E(nc.sbuf_tensor("KVs", [128, 512], BF16))

        bld = B(nc, ctx)
        blockh = E(nc.Block())

        xsems = ["LX", "LXB", "LXC", "LS"]
        x_lds = []
        TQ = T // 4
        for q in range(4):
            x_lds.append(bld.emit("sync", lambda h, q=q: h.dma_start(
                out=S[:, :, q * TQ:(q + 1) * TQ],
                in_=x_h.rearrange("(c p) t -> p c t", p=128)[:, :, q * TQ:(q + 1) * TQ]),
                inc=xsems[q], n=16))
        x_ld = x_lds[3]

        def g_dma(dst, src, waits=None, sem="LG"):
            return bld.emit("gpsimd",
                            lambda h, d=dst, s=src: h.dma_start(out=d, in_=s),
                            waits=waits, inc=sem, n=16)

        g_dma(bqs[:, :], bq_h, sem="LMISC")
        g_dma(bos[:, :], bo_h, sem="LMISC")
        g_dma(gams[:, :], gam_h, sem="LMISC")
        g_dma(bets[:, :], bet_h, sem="LMISC")
        g_dma(gwts[:, :, :], gwt_h.rearrange("(c p) n -> p c n", p=128), sem="LMISC")
        misc_ld = g_dma(idents[:, :], ident_h, sem="LMISC")

        ones_ev = bld.emit("vector", lambda h: h.memset(ones_c[:, :], 1.0), inc="VV")
        ones_ev = bld.emit("vector", lambda h: h.memset(ones_r[:, :], 1.0), inc="VV")
        ones_ev = bld.emit("vector", lambda h: h.memset(ones_c16[:, :], 1.0), inc="VV")
        ones_ev = bld.emit("vector", lambda h: h.memset(ones_r16[:, :], 1.0), inc="VV")
        x16_q = []
        for q in range(4):
            last = None
            for c_ in range(HC):
                last = bld.emit("vector",
                    lambda h, c_=c_, q=q: h.tensor_copy(
                        X16[:, c_, q * TQ:(q + 1) * TQ],
                        S[:, c_, q * TQ:(q + 1) * TQ]),
                    waits=[(xsems[q], x_lds[q])], inc="VV")
            x16_q.append(last)
        x16_ld = x16_q[3]
        x16_h1 = x16_q[1]

        with contextlib.ExitStack() as actx:
            AE = actx.enter_context
            wq16 = AE(nc.sbuf_tensor("wq16", [128, HC, D], BF16))
            wo16 = AE(nc.sbuf_tensor("wo16", [128, HC, D], BF16))

            kvr_ld, endA, wq_ld, wo_ld = _phase_a(
                nc, bld, actx, S, X16, wk_h, wv_h,
                bkr_h, bvr_h, ones_r, ones_ev, x16_q, g_dma,
                cc_in, cc_out, groups, KVs, KV16,
                wq16, wq_h, wo16, wo_h)

            endC = _phase_c(nc, bld, actx, S, X16, wq16, wo16, KV16,
                            bqs, bos, gams, bets, ones_c16, ones_r16,
                            wq_ld, wo_ld, kvr_ld, endA, x_ld, misc_ld)

        if STAGE == 1:
            fin = bld.emit("sync", lambda h: h.dma_start(
                out=y_h.rearrange("(c p) t -> p c t", p=128), in_=S[:, :, :]),
                waits=[("VV", endC["VV"])], inc="LS", n=16)
            bld.emit("sync", lambda h: h.nop(), waits=[("LS", fin)])
        else:
            _ffn(nc, bld, ctx, S, X16, gwts, idents, ones_c16, ones_r16,
                 wg_h, wu_h, wd_h, swg_h, swu_h, swd_h, sel_h, y_h, endC)

        bld.finish(blockh)
    return nc


def _phase_a(nc, bld, actx, S, X16, wk_h, wv_h, bkr_h, bvr_h, ones_r, ones_ev,
             x16_q, g_dma, cc_in, cc_out, groups, KVs, KV16,
             wq16, wq_h, wo16, wo_h):
    with contextlib.ExitStack() as pa:
        P = pa.enter_context
        wk16 = P(nc.sbuf_tensor("wk16", [128, HC, D], BF16))
        wv16 = P(nc.sbuf_tensor("wv16", [128, HC, D], BF16))
        PRE_LOADS = True
        bkrep = P(nc.sbuf_tensor("bkrep", [128, D], F32))
        bvrep = P(nc.sbuf_tensor("bvrep", [128, D], F32))
        kf1 = P(nc.sbuf_tensor("kf1", [128, 2, D], F32))
        kf2 = P(nc.sbuf_tensor("kf2", [128, 2, D], F32))
        kf3 = P(nc.sbuf_tensor("kf3", [128, 2, D], F32))
        ks1 = P(nc.sbuf_tensor("ks1", [128, 2, 16], F32))
        ks4 = P(nc.sbuf_tensor("ks4", [128, 2, 16], F32))
        kn16 = P(nc.sbuf_tensor("kn16", [128, 2, D], BF16))
        vn16 = P(nc.sbuf_tensor("vn16", [128, 2, D], BF16))
        kp = P(nc.psum_tensor("kp", [128, 2, D], F32))
        vp = P(nc.psum_tensor("vp", [128, D], F32))
        KVp = P(nc.psum_tensor("KVp", [128, 512], F32))

        g_dma(kf1[0:1, 0, :], bkr_h, sem="LBK")
        bkv_ld = g_dma(kf1[0:1, 1, :], bvr_h, sem="LBK")
        wk_ld1 = g_dma(wk16[:, 0:4, :],
                       wk_h.rearrange("(c p) n -> p c n", p=128)[:, 0:4, :], sem="LWKH")
        wv_ld1 = g_dma(wv16[:, 0:4, :],
                       wv_h.rearrange("(c p) n -> p c n", p=128)[:, 0:4, :], sem="LWVH")
        wk_ld = g_dma(wk16[:, 4:8, :],
                      wk_h.rearrange("(c p) n -> p c n", p=128)[:, 4:8, :], sem="LWK")
        wv_ld = g_dma(wv16[:, 4:8, :],
                      wv_h.rearrange("(c p) n -> p c n", p=128)[:, 4:8, :], sem="LWV")
        wq_ld = g_dma(wq16[:, :, :], wq_h.rearrange("(c p) n -> p c n", p=128), sem="LWQ")
        wo_ld = g_dma(wo16[:, :, :], wo_h.rearrange("(c p) n -> p c n", p=128), sem="LWO")

        def rep_mm(h, j):
            for half in range(2):
                ins = h.matmul(kp[:, j, half * 512:(half + 1) * 512],
                               ones_r[:, :], kf1[0:1, j, half * 512:(half + 1) * 512],
                               start=True, stop=True)
            return ins
        rb_ev = bld.emit("tensor", lambda h: rep_mm(h, 0),
                         waits=[("LBK", bkv_ld), ("VV", ones_ev)], inc="TT")
        rb_ev2 = bld.emit("tensor", lambda h: rep_mm(h, 1), inc="TT")
        bkrep_ev = bld.emit("vector", lambda h: h.tensor_copy(bkrep[:, :], kp[:, 0, :]),
                            waits=[("TT", rb_ev)], inc="VV")
        bvrep_ev = bld.emit("vector", lambda h: h.tensor_copy(bvrep[:, :], kp[:, 1, :]),
                            waits=[("TT", rb_ev2)], inc="VV")

        kb_ev = {}; vb_ev = {}; kn_ev = {}; kv_ev = {}; sqA_ev = {}
        kv_defer = {}
        for t in range(NT):
            ts0, ts1 = t * 128, (t + 1) * 128

            def kv_proj(h, t=t, ts0=ts0, ts1=ts1, hlo=0, hhi=4):
                for hh in range(hlo, hhi):
                    for half in range(2):
                        h.matmul(kp[:, t % 2, half * 512:(half + 1) * 512],
                                 X16[:, hh, ts0:ts1],
                                 wk16[:, hh, half * 512:(half + 1) * 512],
                                 start=(hh == 0), stop=(hh == HC - 1))
                    for half in range(2):
                        ins = h.matmul(vp[:, half * 512:(half + 1) * 512],
                                       X16[:, hh, ts0:ts1],
                                       wv16[:, hh, half * 512:(half + 1) * 512],
                                       start=(hh == 0), stop=(hh == HC - 1))
                return ins
            w = [("VV", x16_q[t * 4 // NT]),
                 ("LWKH", wk_ld1), ("LWVH", wv_ld1), ("VV", bvrep_ev)]
            if t >= 2:
                w.append(("VV", kb_ev[t - 2]))
            if t >= 1:
                w.append(("VV", vb_ev[t - 1]))
            bld.emit("tensor", kv_proj, waits=w, inc="TT")
            k_ev = bld.emit("tensor",
                            lambda h, kv_proj=kv_proj: kv_proj(h, hlo=4, hhi=8),
                            waits=[("LWK", wk_ld), ("LWV", wv_ld)], inc="TT")
            v_ev = k_ev
            if (t - 2) in kv_defer:
                fn_, tprev = kv_defer.pop(t - 2)
                kv_ev[tprev] = bld.emit("tensor", fn_,
                    waits=[("VV", kn_ev[tprev]), ("VV", vb_ev[tprev])], inc="TT")

            tb = t % 2
            w = [("TT", v_ev)]
            if t >= 2:
                w.append(("TT", kv_ev[t - 2]))
            vb_ev[t] = bld.emit("vector",
                lambda h, tb=tb: h.tensor_tensor(vn16[:, tb, :], vp[:, :], bvrep[:, :], ALU.add),
                waits=w, inc="VV")

            w = [("TT", k_ev)]
            if t >= 2:
                w.append(("SS", sqA_ev[t - 2]))
            kb_ev[t] = bld.emit("vector",
                lambda h, tb=tb: h.tensor_tensor(kf1[:, tb, :], kp[:, tb, :], bkrep[:, :], ALU.add),
                waits=w, inc="VV")
            r_ev = bld.emit("scalar",
                lambda h, tb=tb: h.activation(kf2[:, tb, :], kf1[:, tb, :], AF.Relu),
                waits=[("VV", kb_ev[t])], inc="SS")
            sb_ev = bld.emit("vector",
                lambda h, tb=tb: h.tensor_tensor(kf3[:, tb, :], kf1[:, tb, :], kf2[:, tb, :], ALU.subtract),
                waits=[("SS", r_ev)], inc="VV")
            e_ev = bld.emit("scalar",
                lambda h, tb=tb: h.activation(kf1[:, tb, :], kf3[:, tb, :], AF.Exp),
                waits=[("VV", sb_ev)], inc="SS")
            k1_ev = bld.emit("vector",
                lambda h, tb=tb: h.tensor_tensor(kf3[:, tb, :], kf1[:, tb, :], kf2[:, tb, :], ALU.add),
                waits=[("SS", e_ev)], inc="VV")
            sqA_ev[t] = bld.emit("scalar",
                lambda h, tb=tb: h.activation(kf1[:, tb, :], kf3[:, tb, :], AF.Square),
                waits=[("VV", k1_ev)], inc="SS")
            seg_ev = bld.emit("vector",
                lambda h, tb=tb: h.tensor_reduce(ks1[:, tb, :],
                                          kf1[:, tb, :].rearrange("p (s d) -> p s d", s=16),
                                          AXL.X, ALU.add),
                waits=[("SS", sqA_ev[t])], inc="VV")
            epsA = bld.emit("vector",
                lambda h, tb=tb: h.tensor_scalar(ks4[:, tb, :], ks1[:, tb, :],
                                                 EPS_IN, None, ALU.add),
                waits=[("VV", seg_ev)], inc="VV")
            ln_ev = bld.emit("scalar",
                lambda h, tb=tb: h.activation(ks4[:, tb, :], ks4[:, tb, :], AF.Ln),
                waits=[("VV", epsA)], inc="SS")
            rst_ev = bld.emit("scalar",
                lambda h, tb=tb: h.activation(ks4[:, tb, :], ks4[:, tb, :], AF.Exp, scale=-0.5),
                waits=[("SS", ln_ev)], inc="SS")
            w = [("SS", rst_ev)]
            if t >= 2:
                w.append(("TT", kv_ev[t - 2]))
            kn_ev[t] = bld.emit("vector",
                lambda h, tb=tb: h.tensor_tensor(
                    kn16[:, tb, :].rearrange("p (s d) -> p s d", s=16),
                    kf3[:, tb, :].rearrange("p (s d) -> p s d", s=16),
                    ks4[:, tb, :].unsqueeze(2).broadcast_to([128, 16, 64]),
                    ALU.mult),
                waits=w, inc="VV")

            def kv_mms(h, t=t):
                for hd in range(NH):
                    po = 64 * (hd % 2)
                    fo = 64 * (hd // 2)
                    ins = h.matmul(KVp[po:po + 64, fo:fo + 64],
                                   kn16[:, t % 2, hd * 64:(hd + 1) * 64],
                                   vn16[:, t % 2, hd * 64:(hd + 1) * 64],
                                   start=(t == 0 and hd < 2), stop=(t == NT - 1),
                                   tile_position=(0, po),
                                   skip_group_check=True)
                return ins
            kv_defer[t] = (kv_mms, t)
        for tprev in sorted(kv_defer):
            fn_, tp = kv_defer.pop(tprev)
            kv_ev[tp] = bld.emit("tensor", fn_,
                waits=[("VV", kn_ev[tp]), ("VV", vb_ev[tp])], inc="TT")
        kvs_ev = bld.emit("vector", lambda h: h.tensor_copy(KVs[:, :], KVp[:, :]),
                          waits=[("TT", kv_ev[NT - 1])], inc="VV")
        endA = dict(bld.cnt)
        st_ev = bld.emit("gpsimd", lambda h: h.dma_start(out=cc_in, in_=KVs[:, :]),
                         waits=[("VV", kvs_ev)], inc="LG", n=16)

        cc_ev = bld.emit("gpsimd",
            lambda h: h.collective_compute("AllReduce", ALU.add,
                                           replica_groups=groups,
                                           ins=[cc_in], outs=[cc_out]),
            waits=[("LG", st_ev)], inc="CC")
        kv16_ld = bld.emit("gpsimd", lambda h: h.dma_start(out=KV16[:, :], in_=cc_out),
                           waits=[("CC", cc_ev)], inc="LG", n=16)
        return kv16_ld, endA, wq_ld, wo_ld


def _phase_c(nc, bld, actx, S, X16, wq16, wo16, KV16, bqs, bos, gams, bets,
             ones_c, ones_r, wq_ld, wo_ld, kvr_ld, endA, x_ld, misc_final):
    with contextlib.ExitStack() as pc:
        P = pc.enter_context
        Q16 = P(nc.sbuf_tensor("Q16", [128, 8, CW], BF16))
        A16 = P(nc.sbuf_tensor("A16", [128, HC, CW], BF16))
        Ybuf = P(nc.sbuf_tensor("Ybuf", [128, 2, HC, CW], F32))
        qf1 = P(nc.sbuf_tensor("qf1", [128, 2, CW], F32))
        qf2 = P(nc.sbuf_tensor("qf2", [128, 2, CW], F32))
        qf3 = P(nc.sbuf_tensor("qf3", [128, 2, CW], F32))
        ysc = P(nc.sbuf_tensor("ysc", [128, 2, CW], F32))
        sqb = P(nc.sbuf_tensor("sqb", [128, 2, CW], BF16))
        y16 = P(nc.sbuf_tensor("y16", [128, 2, CW], BF16))
        rows = P(nc.sbuf_tensor("rows", [128, 3, CW], F32))
        rows16 = P(nc.sbuf_tensor("rows16", [1, 2, CW], BF16))
        qp = P(nc.psum_tensor("qp", [128, 2, CW], F32))
        attnp = P(nc.psum_tensor("attnp", [128, CW], F32))
        wop = P(nc.psum_tensor("wop", [128, 2, CW], F32))
        stats = P(nc.psum_tensor("stats", [1, 2, CW], F32))
        rep = P(nc.psum_tensor("rep", [128, CW], F32))

        barrA = [("TT", endA["TT"]), ("VV", endA["VV"]), ("SS", endA["SS"]),
                 ("LWK", endA["LWK"]), ("LWV", endA["LWV"]),
                 ("LWKH", endA["LWKH"]), ("LWVH", endA["LWVH"]),
                 ("LBK", endA["LBK"]),
                 ("LG", endA["LG"]), ("LS", x_ld)]

        def ln_rows(eps, stat_ev):
            ev = bld.emit("vector",
                lambda h: h.tensor_scalar(rows[0:1, 0, :], stats[0:1, 0, :],
                                          -1.0 / D, None, ALU.mult),
                waits=[("TT", stat_ev)], inc="VV")
            stats_free = bld.emit("vector",
                lambda h: h.tensor_scalar(rows[0:1, 1, :], stats[0:1, 1, :],
                                          1.0 / D, None, ALU.mult),
                inc="VV")
            r0c = bld.emit("vector",
                lambda h: h.tensor_copy(rows16[0:1, 0, :], rows[0:1, 0, :]),
                waits=[("VV", stats_free)], inc="VV")
            ev = bld.emit("vector",
                lambda h: h.tensor_tensor(rows[0:1, 2, :], rows[0:1, 0, :],
                                          rows[0:1, 0, :], ALU.mult),
                waits=[("VV", stats_free)], inc="VV")
            ev = bld.emit("vector",
                lambda h: h.tensor_tensor(rows[0:1, 1, :], rows[0:1, 1, :],
                                          rows[0:1, 2, :], ALU.subtract),
                waits=[("VV", ev)], inc="VV")
            ev = bld.emit("vector",
                lambda h, eps=eps: h.tensor_scalar(rows[0:1, 2, :], rows[0:1, 1, :],
                                                   eps, None, ALU.add),
                waits=[("VV", ev)], inc="VV")
            ev = bld.emit("scalar",
                lambda h: h.activation(rows[0:1, 2, :], rows[0:1, 2, :], AF.Ln),
                waits=[("VV", ev)], inc="SS")
            rstd_ev = bld.emit("scalar",
                lambda h: h.activation(rows[0:1, 2, :], rows[0:1, 2, :],
                                       AF.Exp, scale=-0.5),
                waits=[("SS", ev)], inc="SS")
            r2c = bld.emit("vector",
                lambda h: h.tensor_copy(rows16[0:1, 1, :], rows[0:1, 2, :]),
                waits=[("SS", rstd_ev)], inc="VV")
            return stats_free, r0c, r2c

        st = {"stats_free": None, "rep_free": None, "prev_stat2": None,
              "last_x16": None}
        qp_guard = {}
        q1_ev = {}
        attn_ev = {}
        atev_ev = {}
        y_evs = {}
        ln_y_free = {}

        def emit_qaw(u):
            ub = u % 2
            us0, us1 = u * CW, (u + 1) * CW
            for c in range(HC):
                k = u * HC + c

                def q_mms(h, c=c, us0=us0, us1=us1):
                    for hh in range(HC):
                        ins = h.matmul(qp[:, c % 2, :],
                                       wq16[:, hh, c * 128:(c + 1) * 128],
                                       X16[:, hh, us0:us1],
                                       start=(hh == 0), stop=(hh == HC - 1))
                    return ins
                w = list(barrA) + [("LWQ", wq_ld)]
                if k >= 2:
                    w.append(("VV", qp_guard[k - 2]))
                qpe = bld.emit("tensor", q_mms, waits=w, inc="TT")

                w = [("TT", qpe), ("LMISC", misc_final)]
                if k >= 2:
                    w.append(("VV", q1_ev[k - 2]))
                rq = bld.emit("scalar",
                    lambda h, c=c: h.activation(qf1[:, c % 2, :], qp[:, c % 2, :],
                                                AF.Relu, bias=bqs[:, c:c + 1]),
                    waits=w, inc="SS")
                mq = bld.emit("vector",
                    lambda h, c=c: h.scalar_tensor_tensor(
                        qf2[:, c % 2, :], qp[:, c % 2, :], bqs[:, c:c + 1],
                        qf1[:, c % 2, :], ALU.add, ALU.subtract),
                    waits=[("SS", rq), ("LMISC", misc_final)], inc="VV")
                qp_guard[k] = mq
                eq = bld.emit("scalar",
                    lambda h, c=c: h.activation(qf3[:, c % 2, :], qf2[:, c % 2, :], AF.Exp),
                    waits=[("VV", mq)], inc="SS")
                w = [("SS", eq)]
                if k >= 8:
                    w.append(("TT", attn_ev[k - 8]))
                q1_ev[k] = bld.emit("vector",
                    lambda h, k=k, c=c: h.tensor_tensor(Q16[:, k % 8, :], qf1[:, c % 2, :],
                                                        qf3[:, c % 2, :], ALU.add),
                    waits=w, inc="VV")

                def emit_attn(kk):
                    cc_ = kk % HC

                    def a_mms(h, kk=kk, cc_=cc_):
                        h.matmul(attnp[0:64, :], KV16[0:64, cc_ * 64:(cc_ + 1) * 64],
                                 Q16[0:64, kk % 8, :], start=True, stop=True)
                        return h.matmul(attnp[64:128, :], KV16[64:128, cc_ * 64:(cc_ + 1) * 64],
                                        Q16[64:128, kk % 8, :], start=True, stop=True,
                                        tile_position=(64, 64))
                    w = [("VV", q1_ev[kk]), ("LG", kvr_ld)]
                    if kk >= 1:
                        w.append(("SS", atev_ev[kk - 1]))
                    attn_ev[kk] = bld.emit("tensor", a_mms, waits=w, inc="TT")
                    atev_ev[kk] = bld.emit("scalar",
                        lambda h, cc_=cc_: h.activation(A16[:, cc_, :], attnp[:, :], AF.Copy),
                        waits=[("TT", attn_ev[kk])], inc="SS")
                yield

            for kk in range(u * HC, u * HC + HC):
                emit_attn(kk)
            yield

            y_ev = {}
            yb_guard = {}
            for o in range(HC):
                def wo_mms(h, o=o):
                    for cc2 in range(HC):
                        ins = h.matmul(wop[:, o % 2, :],
                                       wo16[:, cc2, o * 128:(o + 1) * 128],
                                       A16[:, cc2, :],
                                       start=(cc2 == 0), stop=(cc2 == HC - 1))
                    return ins
                w = [("SS", atev_ev[u * HC + HC - 1]), ("LWO", wo_ld)]
                if o >= 2:
                    w.append(("SS", yb_guard[o - 2]))
                woe = bld.emit("tensor", wo_mms, waits=w, inc="TT")
                w = [("TT", woe), ("LMISC", misc_final)]
                if o >= 2:
                    w.append(("VV", y_ev[o - 2]))
                if o < 2 and u >= 1:
                    w.append(("VV", y_evs[u - 1][HC - 2 + o]))
                yb_guard[o] = bld.emit("scalar",
                    lambda h, o=o: h.activation(ysc[:, o % 2, :], wop[:, o % 2, :],
                                                AF.Identity, bias=bos[:, o:o + 1]),
                    waits=w, inc="SS")
                w = [("SS", yb_guard[o])]
                if u >= 2:
                    w.append(("VV", ln_y_free[u - 2]))
                y_ev[o] = bld.emit("vector",
                    lambda h, o=o, ub=ub, us0=us0, us1=us1: h.tensor_tensor(
                        Ybuf[:, ub, o, :], ysc[:, o % 2, :], S[:, o, us0:us1], ALU.add),
                    waits=w, inc="VV")
                yield
            y_evs[u] = y_ev

        def emit_ln(u):
            ub = u % 2
            us0, us1 = u * CW, (u + 1) * CW
            y_ev = y_evs[u]
            stat_g = {}
            for c in range(HC):
                w = [("VV", y_ev[c])]
                if c >= 2:
                    w.append(("TT", stat_g[c - 2]))
                elif st["prev_stat2"] is not None:
                    w.append(("TT", st["prev_stat2"][HC - 2 + c]))
                sqe = bld.emit("scalar",
                    lambda h, c=c, ub=ub: h.activation(sqb[:, c % 2, :],
                                                       Ybuf[:, ub, c, :], AF.Square),
                    waits=w, inc="SS")
                y16e = bld.emit("gpsimd",
                    lambda h, c=c, ub=ub: h.tensor_copy(y16[:, c % 2, :],
                                                        Ybuf[:, ub, c, :]),
                    waits=w, inc="GG")

                def st_mms(h, c=c, ub=ub):
                    h.matmul(stats[0:1, 0, :], ones_c[:, :], y16[:, c % 2, :],
                             start=(c == 0), stop=(c == HC - 1))
                    return h.matmul(stats[0:1, 1, :], ones_c[:, :], sqb[:, c % 2, :],
                                    start=(c == 0), stop=(c == HC - 1))
                w = [("SS", sqe), ("GG", y16e), ("VV", y_ev[c])]
                if c == 0 and st["stats_free"] is not None:
                    w.append(("VV", st["stats_free"]))
                stat_g[c] = bld.emit("tensor", st_mms, waits=w, inc="TT")
                yield

            stats_free, r0c, r2c = ln_rows(EPS_IN, stat_g[HC - 1])

            w = [("VV", r0c)]
            if st["rep_free"] is not None:
                w.append(("VV", st["rep_free"]))
            rme = bld.emit("tensor",
                lambda h: h.matmul(rep[:, :], ones_r[:, :], rows16[0:1, 0, :],
                                   start=True, stop=True),
                waits=w, inc="TT")
            yield
            ev = None
            for c in range(HC):
                ev = bld.emit("vector",
                    lambda h, c=c, ub=ub: h.tensor_tensor(Ybuf[:, ub, c, :],
                                                          Ybuf[:, ub, c, :],
                                                          rep[:, :], ALU.add),
                    waits=[("TT", rme)], inc="VV")
            rre = bld.emit("tensor",
                lambda h: h.matmul(rep[:, :], ones_r[:, :], rows16[0:1, 1, :],
                                   start=True, stop=True),
                waits=[("VV", ev), ("VV", r2c)], inc="TT")
            yield
            for c in range(HC):
                ev = bld.emit("vector",
                    lambda h, c=c, ub=ub: h.tensor_tensor(Ybuf[:, ub, c, :],
                                                          Ybuf[:, ub, c, :],
                                                          rep[:, :], ALU.mult),
                    waits=[("TT", rre)], inc="VV")
            gb_ev = None
            for c in range(HC):
                gb_ev = bld.emit("scalar",
                    lambda h, c=c, ub=ub: h.activation(Ybuf[:, ub, c, :],
                                                       Ybuf[:, ub, c, :], AF.Identity,
                                                       bias=bets[:, c:c + 1],
                                                       scale=gams[:, c:c + 1]),
                    waits=[("VV", ev), ("LMISC", misc_final)], inc="SS")
            for c in range(HC):
                ev = bld.emit("vector",
                    lambda h, c=c, ub=ub, us0=us0, us1=us1: h.tensor_tensor(
                        Ybuf[:, ub, c, :], Ybuf[:, ub, c, :], S[:, c, us0:us1], ALU.add),
                    waits=[("SS", gb_ev)], inc="VV")
            w_done = ev

            stat2_g = {}
            for c in range(HC):
                w = [("VV", w_done)]
                if c >= 2:
                    w.append(("TT", stat2_g[c - 2]))
                sqe = bld.emit("scalar",
                    lambda h, c=c, ub=ub: h.activation(sqb[:, c % 2, :],
                                                       Ybuf[:, ub, c, :], AF.Square),
                    waits=w, inc="SS")
                y16e = bld.emit("gpsimd",
                    lambda h, c=c, ub=ub: h.tensor_copy(y16[:, c % 2, :],
                                                        Ybuf[:, ub, c, :]),
                    waits=w, inc="GG")

                def st2_mms(h, c=c, ub=ub):
                    h.matmul(stats[0:1, 0, :], ones_c[:, :], y16[:, c % 2, :],
                             start=(c == 0), stop=(c == HC - 1))
                    return h.matmul(stats[0:1, 1, :], ones_c[:, :], sqb[:, c % 2, :],
                                    start=(c == 0), stop=(c == HC - 1))
                w = [("SS", sqe), ("GG", y16e), ("VV", w_done)]
                if c == 0:
                    w.append(("VV", stats_free))
                stat2_g[c] = bld.emit("tensor", st2_mms, waits=w, inc="TT")
                yield
            st["prev_stat2"] = stat2_g

            stats_free, r0c, r2c = ln_rows(EPS_OUT, stat2_g[HC - 1])
            st["stats_free"] = stats_free

            rme2 = bld.emit("tensor",
                lambda h: h.matmul(rep[:, :], ones_r[:, :], rows16[0:1, 0, :],
                                   start=True, stop=True),
                waits=[("VV", r0c), ("VV", ev)], inc="TT")
            yield
            for c in range(HC):
                ev = bld.emit("vector",
                    lambda h, c=c, ub=ub: h.tensor_tensor(Ybuf[:, ub, c, :],
                                                          Ybuf[:, ub, c, :],
                                                          rep[:, :], ALU.add),
                    waits=[("TT", rme2)], inc="VV")
            rre2 = bld.emit("tensor",
                lambda h: h.matmul(rep[:, :], ones_r[:, :], rows16[0:1, 1, :],
                                   start=True, stop=True),
                waits=[("VV", ev), ("VV", r2c)], inc="TT")
            yield
            for c in range(HC):
                ev = bld.emit("vector",
                    lambda h, c=c, ub=ub, us0=us0, us1=us1: h.tensor_tensor(
                        S[:, c, us0:us1], Ybuf[:, ub, c, :], rep[:, :], ALU.mult),
                    waits=[("TT", rre2)], inc="VV")
            st["rep_free"] = ev
            ln_y_free[u] = ev
            xc_ev = None
            for c in range(HC):
                xc_ev = bld.emit("scalar",
                    lambda h, c=c, us0=us0, us1=us1: h.copy(
                        X16[:, c, us0:us1], S[:, c, us0:us1]),
                    waits=[("VV", ev)], inc="SS")
            st["last_x16"] = xc_ev

        pending = []

        def pump(k):
            while k > 0 and pending:
                try:
                    next(pending[0])
                    k -= 1
                except StopIteration:
                    pending.pop(0)

        SCHED = [0, 1, 1, 1, 1, 1, 1, 1,
                 1,
                 1, 1, 0, 0, 0, 2, 2, 2]
        for u in range(NTC):
            qg = emit_qaw(u)
            pi = 0
            for _ in qg:
                if pi == 8:
                    while len(pending) > 1:
                        try:
                            next(pending[0])
                        except StopIteration:
                            pending.pop(0)
                pump(SCHED[pi] if pi < len(SCHED) else 1)
                pi += 1
            pending.append(emit_ln(u))
        while pending:
            try:
                next(pending[0])
            except StopIteration:
                pending.pop(0)
        return dict(bld.cnt)


def _ffn(nc, bld, ctx, S, X16, gwts, idents, ones_c, ones_r,
         wg_h, wu_h, wd_h, swg_h, swu_h, swd_h, sel_h, y_h, endC):
    E = ctx.enter_context
    ring = E(nc.sbuf_tensor("ring", [128, RING, 3, 1024], BF16))
    act = E(nc.sbuf_tensor("act", [128, 2, SW], BF16))
    actm = E(nc.sbuf_tensor("actm", [128, 2, SW], BF16))
    s1t = E(nc.sbuf_tensor("s1t", [128, 2, SW], BF16))
    w8sb = E(nc.sbuf_tensor("w8sb", [8, T], BF16))
    sel8 = E(nc.sbuf_tensor("sel8", [8, D], F32))
    sel16 = E(nc.sbuf_tensor("sel16", [8, D], BF16))
    w_tm = E(nc.sbuf_tensor("w_tm", [128, NT, 8], F32))
    ysq = E(nc.sbuf_tensor("ysq", [128, 2, SW], BF16))
    y16F = E(nc.sbuf_tensor("y16F", [128, 2, SW], BF16))
    rowsF = E(nc.sbuf_tensor("rowsF", [128, 3, SW], F32))
    rowsF16 = E(nc.sbuf_tensor("rowsF16", [1, 2, SW], BF16))
    wbs = E(nc.sbuf_tensor("wbs", [128, 2, SW], F32))
    pb1 = E(nc.sbuf_tensor("pb1", [128, NT, 8], F32))
    pb2 = E(nc.sbuf_tensor("pb2", [128, NT, 8], F32))
    pb3 = E(nc.sbuf_tensor("pb3", [128, NT, 8], F32))
    rt = E(nc.sbuf_tensor("rt", [128, NT, 6], F32))
    pairp = E(nc.psum_tensor("pairp", [128, 2, 512], F32))
    dnp = E(nc.psum_tensor("dnp", [128, HC, SW], F32))
    mfp = E(nc.psum_tensor("mfp", [128, 512], F32))
    gatep = E(nc.psum_tensor("gatep", [128, 512], F32))

    barrC = [("TT", endC["TT"]), ("VV", endC["VV"]), ("SS", endC["SS"]),
             ("LWQ", endC["LWQ"]), ("LWO", endC["LWO"]), ("LMISC", endC["LMISC"]),
             ("LG", endC["LG"]), ("LS", endC["LS"])]

    sel_ld = bld.emit("gpsimd", lambda h: h.dma_start(out=sel8[:, :], in_=sel_h),
                      waits=barrC, inc="LSEL", n=16)
    sel16_ev = bld.emit("vector", lambda h: h.tensor_copy(sel16[:, :], sel8[:, :]),
                        waits=[("LSEL", sel_ld)], inc="VV")

    rst = {"w8": None}
    lg_prev = None
    pend_tr = None
    for t in range(NT):
        ts0, ts1 = t * 128, (t + 1) * 128

        def g_mms(h, ts0=ts0, ts1=ts1):
            for hh in range(HC):
                ins = h.matmul(gatep[:, 0:8], S[:, hh, ts0:ts1], gwts[:, hh, :],
                               start=(hh == 0), stop=(hh == HC - 1))
            return ins
        w = list(barrC)
        if lg_prev is not None:
            w.append(("VV", lg_prev))
        g_ev = bld.emit("tensor", g_mms, waits=w, inc="TT")
        lg_ev = bld.emit("vector",
            lambda h, t=t: h.tensor_copy(w_tm[:, t, :], gatep[:, 0:8]),
            waits=[("TT", g_ev)], inc="VV")
        lg_prev = lg_ev
        if pend_tr is not None:
            pend_tr()

        ev = bld.emit("vector",
            lambda h, t=t: h.reduce_max(rt[:, t, 0:1], w_tm[:, t, :], AXL.X),
            waits=[("VV", lg_ev)], inc="VV")
        ev = bld.emit("vector",
            lambda h, t=t: h.tensor_scalar(rt[:, t, 1:2], rt[:, t, 0:1], -1.0, None, ALU.mult),
            waits=[("VV", ev)], inc="VV")
        pexp = bld.emit("scalar",
            lambda h, t=t: h.activation(pb1[:, t, :], w_tm[:, t, :], AF.Exp, bias=rt[:, t, 1:2]),
            waits=[("VV", ev)], inc="SS")
        zs = bld.emit("vector",
            lambda h, t=t: h.reduce_sum(rt[:, t, 2:3], pb1[:, t, :], AXL.X),
            waits=[("SS", pexp)], inc="VV")
        ev = bld.emit("vector",
            lambda h, t=t: h.reduce_max(rt[:, t, 3:4], pb1[:, t, :], AXL.X),
            waits=[("VV", zs)], inc="VV")
        ev = bld.emit("vector",
            lambda h, t=t: h.tensor_scalar(pb2[:, t, :], pb1[:, t, :], rt[:, t, 3:4], None, ALU.is_ge),
            waits=[("VV", ev)], inc="VV")
        ev = bld.emit("vector",
            lambda h, t=t: h.scalar_tensor_tensor(pb3[:, t, :], pb2[:, t, :], -2.0, pb1[:, t, :],
                                             ALU.mult, ALU.add),
            waits=[("VV", ev)], inc="VV")
        ev = bld.emit("vector",
            lambda h, t=t: h.reduce_max(rt[:, t, 4:5], pb3[:, t, :], AXL.X),
            waits=[("VV", ev)], inc="VV")
        ev = bld.emit("vector",
            lambda h, t=t: h.tensor_scalar(pb2[:, t, :], pb1[:, t, :], rt[:, t, 4:5], None, ALU.is_ge),
            waits=[("VV", ev)], inc="VV")
        ev = bld.emit("vector",
            lambda h, t=t: h.tensor_tensor(pb3[:, t, :], pb1[:, t, :], pb2[:, t, :], ALU.mult),
            waits=[("VV", ev)], inc="VV")
        ev = bld.emit("vector",
            lambda h, t=t: h.reduce_sum(rt[:, t, 5:6], pb3[:, t, :], AXL.X),
            waits=[("VV", ev)], inc="VV")
        ev = bld.emit("vector",
            lambda h, t=t: h.scalar_tensor_tensor(rt[:, t, 0:1], rt[:, t, 2:3], 1e-20, rt[:, t, 5:6],
                                             ALU.mult, ALU.add),
            waits=[("VV", ev)], inc="VV")
        ev = bld.emit("vector",
            lambda h, t=t: h.reciprocal(rt[:, t, 1:2], rt[:, t, 0:1]),
            waits=[("VV", ev)], inc="VV")
        wt_ev = bld.emit("vector",
            lambda h, t=t: h.tensor_scalar(w_tm[:, t, :], pb3[:, t, :], rt[:, t, 1:2], None, ALU.mult),
            waits=[("VV", ev)], inc="VV")

        def mk_tr(t=t, ts0=ts0, ts1=ts1, wt_ev=wt_ev):
            def do():
                w = [("VV", wt_ev)]
                if rst["w8"] is not None:
                    w.append(("VV", rst["w8"]))
                tr_ev = bld.emit("tensor",
                    lambda h, t=t: h.transpose(mfp[0:8, 128:256], w_tm[:, t, :],
                                               idents[:, :]),
                    waits=w, inc="TT")
                rst["w8"] = bld.emit("vector",
                    lambda h, ts0=ts0, ts1=ts1: h.tensor_copy(w8sb[0:8, ts0:ts1],
                                                              mfp[0:8, 128:256]),
                    waits=[("TT", tr_ev)], inc="VV")
            return do
        pend_tr = mk_tr()
    pend_tr()
    routing_done = rst["w8"]

    nsets = NU * IC
    ld_at = {}
    for s in range(nsets):
        ld_at[s] = 48 * (s // RING + 1)
    for j in range(RING):
        bld.cnt[f"LW{j}"] = 48 * ((nsets - 1 - j) // RING + 1)

    set_last_use = {}
    pair_s = {}
    pair_v = {}
    dn_evs = {}
    vy_ev = {}
    wb_ev = {}

    def emit_dn(u, w_, i, k, b, slot, vy_prev):
        ws0, ws1 = w_ * SW, (w_ + 1) * SW

        def dnf(h, i=i, b=b, slot=slot):
            for o in range(HC):
                ins = h.matmul(dnp[:, o, :], ring[:, slot, 2, o * 128:(o + 1) * 128],
                               act[:, b, :],
                               start=(i == 0 and o % 2 == 0),
                               stop=(i == IC - 1 and o % 2 == 1))
            return ins
        wts = [("VV", pair_v[k])]
        if i == 0 and vy_prev is not None:
            wts.append(("VV", vy_prev))
        if w_ == 0:
            s_ = u * IC + i
            wts.append((f"LW{s_ % RING}", ld_at[s_]))
        dn_evs[k] = bld.emit("tensor", dnf, waits=wts, inc="TT")
        if w_ == NSW - 1:
            set_last_use[u * IC + i] = dn_evs[k]

    kglob = 0
    vy_prev = None
    av_last = {}
    wbs_ev = {}
    for u in range(NU):
        for w_ in range(NSW):
            g = u * NSW + w_
            ws0, ws1 = w_ * SW, (w_ + 1) * SW
            if u < 8:
                wts = [("VV", routing_done), ("VV", sel16_ev)]
                if g >= 2 and (g - 2) in av_last:
                    wts.append(("VV", av_last[g - 2]))
                if (u, w_) != (0, 0):
                    pass
                wb_ev[(u, w_)] = bld.emit("tensor",
                    lambda h, u=u, ws0=ws0, ws1=ws1: h.matmul(
                        mfp[:, 0:SW], sel16[:, u * 128:(u + 1) * 128],
                        w8sb[:, ws0:ws1], start=True, stop=True),
                    waits=wts, inc="TT")
                wbs_ev[g] = bld.emit("scalar",
                    lambda h, g=g: h.activation(wbs[:, g % 2, :], mfp[:, 0:SW], AF.Copy),
                    waits=[("TT", wb_ev[(u, w_)])] +
                          ([("VV", av_last[g - 2])] if (g - 2) in av_last else []),
                    inc="SS")

            pend = None
            for i in range(IC):
                k = kglob
                kglob += 1
                b = k % 2
                slot = (u * IC + i) % RING

                def p1(h, b=b, slot=slot, ws0=ws0, ws1=ws1):
                    for hh in range(HC):
                        h.matmul(pairp[:, b, 0:SW],
                                 ring[:, slot, 0, hh * 128:(hh + 1) * 128],
                                 X16[:, hh, ws0:ws1],
                                 start=(hh == 0), stop=False)
                    for hh in range(HC):
                        ins = h.matmul(pairp[:, b, SW:2 * SW],
                                       ring[:, slot, 1, hh * 128:(hh + 1) * 128],
                                       X16[:, hh, ws0:ws1],
                                       start=False, stop=(hh == HC - 1))
                    return ins
                wts = []
                if k >= 2:
                    wts += [("SS", pair_s[k - 2]), ("VV", pair_v[k - 2])]
                if w_ == 0:
                    s_ = u * IC + i
                    wts.append((f"LW{s_ % RING}", ld_at[s_]))
                if u == 0 and w_ == 0 and i == 0:
                    wts.append(("VV", routing_done))
                p1e = bld.emit("tensor", p1, waits=wts, inc="TT")
                if pend is not None:
                    emit_dn(*pend)

                wts = [("TT", p1e)]
                if k >= 2:
                    wts.append(("VV", pair_v[k - 2]))
                pair_s[k] = bld.emit("scalar",
                    lambda h, b=b: h.activation(s1t[:, b, :], pairp[:, b, 0:SW], AF.Silu),
                    waits=wts, inc="SS")
                if u < 8:
                    wts = [("SS", pair_s[k])]
                    if k >= 2:
                        wts.append(("TT", dn_evs[k - 2]))
                    mid = bld.emit("vector",
                        lambda h, b=b: h.tensor_tensor(actm[:, b, :], s1t[:, b, :],
                                                       pairp[:, b, SW:2 * SW], ALU.mult),
                        waits=wts, inc="VV")
                    pair_v[k] = bld.emit("vector",
                        lambda h, b=b, g=g: h.tensor_tensor(act[:, b, :], actm[:, b, :],
                                                            wbs[:, g % 2, :], ALU.mult),
                        waits=[("VV", mid), ("SS", wbs_ev[g])], inc="VV")
                else:
                    wts = [("SS", pair_s[k])]
                    if k >= 2:
                        wts.append(("TT", dn_evs[k - 2]))
                    pair_v[k] = bld.emit("vector",
                        lambda h, b=b: h.tensor_tensor(act[:, b, :], s1t[:, b, :],
                                                       pairp[:, b, SW:2 * SW], ALU.mult),
                        waits=wts, inc="VV")
                pend = (u, w_, i, k, b, slot, vy_prev)
            emit_dn(*pend)
            av_last[g] = pair_v[kglob - 1]

            ev = None
            for o in range(HC):
                wts = []
                if o == 0:
                    wts.append(("TT", dn_evs[kglob - 1]))
                ev = bld.emit("vector",
                    lambda h, o=o, ws0=ws0, ws1=ws1: h.tensor_tensor(
                        S[:, o, ws0:ws1], S[:, o, ws0:ws1], dnp[:, o, :], ALU.add),
                    waits=wts, inc="VV")
            vy_ev[(u, w_)] = ev
            vy_prev = ev

    def src_aps(u, i):
        if u < 8:
            return (wg_h[u].rearrange("(c p) n -> p c n", p=128)[:, :, i * 128:(i + 1) * 128],
                    wu_h[u].rearrange("(c p) n -> p c n", p=128)[:, :, i * 128:(i + 1) * 128],
                    wd_h[u][i * 128:(i + 1) * 128, :])
        off = (u - 8) * IE
        return (swg_h.rearrange("(c p) n -> p c n", p=128)[:, :, off + i * 128:off + (i + 1) * 128],
                swu_h.rearrange("(c p) n -> p c n", p=128)[:, :, off + i * 128:off + (i + 1) * 128],
                swd_h[off + i * 128:off + (i + 1) * 128, :])

    for s in range(nsets):
        u, i = s // IC, s % IC
        slot = s % RING
        gsrc, usrc, dsrc = src_aps(u, i)
        wts = list(barrC) if s < RING else [("TT", set_last_use[s - RING])]
        lw = f"LW{slot}"
        bld.emit("gpsimd", lambda h, slot=slot, gsrc=gsrc: h.dma_start(
            out=ring[:, slot, 0, :].rearrange("p (c n) -> p c n", c=HC), in_=gsrc),
            waits=wts, inc=lw, n=16, reserved=True)
        bld.emit("gpsimd", lambda h, slot=slot, usrc=usrc: h.dma_start(
            out=ring[:, slot, 1, :].rearrange("p (c n) -> p c n", c=HC), in_=usrc),
            inc=lw, n=16, reserved=True)
        bld.emit("gpsimd", lambda h, slot=slot, dsrc=dsrc: h.dma_start(
            out=ring[:, slot, 2, :], in_=dsrc), inc=lw, n=16, reserved=True)

    out_ld = None
    x2_done = None
    prev_fst_last = None
    prev_mfp_free = None
    for w_ in range(NSW):
        ws0, ws1 = w_ * SW, (w_ + 1) * SW
        fst_g = {}
        for c in range(HC):
            w = [("VV", vy_ev[(NU - 1, w_)])]
            if c >= 2:
                w.append(("TT", fst_g[c - 2]))
            elif prev_fst_last is not None:
                w.append(("TT", prev_fst_last))
            sqe = bld.emit("scalar",
                lambda h, c=c, ws0=ws0, ws1=ws1: h.activation(
                    ysq[:, c % 2, :], S[:, c, ws0:ws1], AF.Square),
                waits=w, inc="SS")
            y16e = bld.emit("scalar",
                lambda h, c=c, ws0=ws0, ws1=ws1: h.copy(
                    y16F[:, c % 2, :], S[:, c, ws0:ws1]),
                inc="SS")

            def fst_mms(h, c=c, ws0=ws0, ws1=ws1):
                h.matmul(mfp[0:1, 256:512], ones_c[:, :], y16F[:, c % 2, :],
                         start=(c == 0), stop=False)
                return h.matmul(mfp[0:1, 0:256], ones_c[:, :], ysq[:, c % 2, :],
                                start=False, stop=(c == HC - 1))
            w = [("SS", y16e), ("VV", vy_ev[(NU - 1, w_)])]
            if c == 0 and prev_mfp_free is not None:
                w.append(("VV", prev_mfp_free))
            fst_g[c] = bld.emit("tensor", fst_mms, waits=w, inc="TT")
        prev_fst_last = fst_g[HC - 1]

        ev = bld.emit("vector",
            lambda h: h.tensor_scalar(rowsF[0:1, 0, :], mfp[0:1, 256:512],
                                      -1.0 / D, None, ALU.mult),
            waits=[("TT", fst_g[HC - 1])], inc="VV")
        ev = bld.emit("vector",
            lambda h: h.tensor_scalar(rowsF[0:1, 1, :], mfp[0:1, 0:256],
                                      1.0 / D, None, ALU.mult),
            inc="VV")
        mfp_free = ev
        prev_mfp_free = ev
        rF0c = bld.emit("vector",
            lambda h: h.tensor_copy(rowsF16[0:1, 0, :], rowsF[0:1, 0, :]),
            waits=[("VV", mfp_free)], inc="VV")
        ev = bld.emit("vector",
            lambda h: h.tensor_tensor(rowsF[0:1, 2, :], rowsF[0:1, 0, :],
                                      rowsF[0:1, 0, :], ALU.mult),
            waits=[("VV", ev)], inc="VV")
        ev = bld.emit("vector",
            lambda h: h.tensor_tensor(rowsF[0:1, 1, :], rowsF[0:1, 1, :],
                                      rowsF[0:1, 2, :], ALU.subtract),
            waits=[("VV", ev)], inc="VV")
        ev = bld.emit("vector",
            lambda h: h.tensor_scalar(rowsF[0:1, 2, :], rowsF[0:1, 1, :],
                                      EPS_OUT, None, ALU.add),
            waits=[("VV", ev)], inc="VV")
        ev = bld.emit("scalar",
            lambda h: h.activation(rowsF[0:1, 2, :], rowsF[0:1, 2, :], AF.Ln),
            waits=[("VV", ev)], inc="SS")
        rste = bld.emit("scalar",
            lambda h: h.activation(rowsF[0:1, 2, :], rowsF[0:1, 2, :],
                                   AF.Exp, scale=-0.5),
            waits=[("SS", ev)], inc="SS")
        rF2c = bld.emit("vector",
            lambda h: h.tensor_copy(rowsF16[0:1, 1, :], rowsF[0:1, 2, :]),
            waits=[("SS", rste)], inc="VV")

        w = [("VV", rF0c)]
        if x2_done is not None:
            w.append(("VV", x2_done))
        rme = bld.emit("tensor",
            lambda h: h.matmul(gatep[:, 0:SW], ones_r[:, :], rowsF16[0:1, 0, :],
                               start=True, stop=False),
            waits=w, inc="TT")
        rre = bld.emit("tensor",
            lambda h: h.matmul(gatep[:, SW:2 * SW], ones_r[:, :], rowsF16[0:1, 1, :],
                               start=False, stop=True),
            waits=[("VV", rF2c)], inc="TT")
        ev = None
        for c in range(HC):
            ev = bld.emit("vector",
                lambda h, c=c, ws0=ws0, ws1=ws1: h.tensor_tensor(
                    S[:, c, ws0:ws1], S[:, c, ws0:ws1], gatep[:, 0:SW], ALU.add),
                waits=[("TT", rre)], inc="VV")
        for c in range(HC):
            ev = bld.emit("vector",
                lambda h, c=c, ws0=ws0, ws1=ws1: h.tensor_tensor(
                    S[:, c, ws0:ws1], S[:, c, ws0:ws1], gatep[:, SW:2 * SW], ALU.mult),
                waits=[("TT", rre), ("VV", ev)], inc="VV")
        x2_done = ev
        out_ld = bld.emit("sync", lambda h, ws0=ws0, ws1=ws1: h.dma_start(
            out=y_h.rearrange("(c p) t -> p c t", p=128)[:, :, ws0:ws1],
            in_=S[:, :, ws0:ws1]),
            waits=[("VV", x2_done)], inc="LS", n=16)

    bld.emit("sync", lambda h: h.nop(), waits=[("LS", out_ld)])


def kernel(L0_attn_x, L0_attn_Wq, L0_attn_bq, L0_attn_Wk, L0_attn_bk,
           L0_attn_Wv, L0_attn_bv, L0_attn_Wo, L0_attn_bo, L0_attn_gamma,
           L0_attn_beta, L0_ffn_gate_w, L0_ffn_wg, L0_ffn_wu, L0_ffn_wd,
           L0_ffn_swg, L0_ffn_swu, L0_ffn_swd):
    np32 = lambda a: np.ascontiguousarray(np.asarray(a, dtype=np.float32))
    x = np32(L0_attn_x).reshape(NCORES * T, D)
    fm = lambda b: np.ascontiguousarray(np32(b).reshape(HC, 128).T)
    sel = np.zeros((8, D), np.float32)
    for e in range(8):
        sel[e, e * 128:(e + 1) * 128] = 1.0
    common = {
        "wq": np32(L0_attn_Wq), "wk": np32(L0_attn_Wk),
        "wv": np32(L0_attn_Wv), "wo": np32(L0_attn_Wo),
        "bq": fm(L0_attn_bq), "bo": fm(L0_attn_bo),
        "gam": fm(L0_attn_gamma), "bet": fm(L0_attn_beta),
        "bkr": np32(L0_attn_bk).reshape(1, D), "bvr": np32(L0_attn_bv).reshape(1, D),
        "gwt": np.ascontiguousarray(np32(L0_ffn_gate_w).T),
        "ident": np.eye(128, dtype=np.float32),
        "sel": sel,
        "wg": np32(L0_ffn_wg), "wu": np32(L0_ffn_wu), "wd": np32(L0_ffn_wd),
        "swg": np32(L0_ffn_swg), "swu": np32(L0_ffn_swu), "swd": np32(L0_ffn_swd),
    }
    in_maps = []
    for c in range(NCORES):
        m = dict(common)
        m["x"] = np.ascontiguousarray(x[c * T:(c + 1) * T, :].T)
        in_maps.append(m)
    nc = build()
    res = run_bass_kernel_spmd(nc, in_maps, core_ids=list(range(NCORES)))
    outs = [np.asarray(res.results[c]["y"]).reshape(D, T).T for c in range(NCORES)]
    return np.concatenate(outs, axis=0).reshape(2, NCORES * T // 2, D).astype(np.float32)
